# version 23
# baseline (speedup 1.0000x reference)
import numpy as np
from contextlib import ExitStack

import concourse.bass as bass
import concourse.mybir as mybir
from concourse.bass_utils import run_bass_kernel_spmd

F32 = mybir.dt.float32
BF16 = mybir.dt.bfloat16
AF = mybir.ActivationFunctionType
ALU = mybir.AluOpType
AX = mybir.AxisListType

D = 2048
DC = 16
TPC = 2048
TILE = 512
NT = TPC // TILE
NEXP = 12
JDIM = 1408
JC = JDIM // 128
EPS = 1e-5

ENGINES = ("sync", "tensor", "vector", "scalar", "gpsimd")


class Prog:

    def __init__(self):
        self.ops = {e: [] for e in ENGINES}

    def run(self, nc):
        with nc.Block() as blk:
            @blk.sync
            def _(e):
                for f in self.ops["sync"]:
                    f(e)

            @blk.tensor
            def _(e):
                for f in self.ops["tensor"]:
                    f(e)

            @blk.vector
            def _(e):
                for f in self.ops["vector"]:
                    f(e)

            @blk.scalar
            def _(e):
                for f in self.ops["scalar"]:
                    f(e)

            @blk.gpsimd
            def _(e):
                for f in self.ops["gpsimd"]:
                    f(e)


def build_program(debug=False):
    nc = bass.Bass("TRN2", target_bir_lowering=False)

    xT = nc.dram_tensor("xT", [NT, 128, DC * TILE], F32,
                        kind="ExternalInput").ap()
    xlead_in = nc.dram_tensor("xlead_in", [128, DC], F32, kind="ExternalInput").ap()
    wT_in = {w: nc.dram_tensor(f"w{w}T", [8, 128, DC * 256], F32,
                               kind="ExternalInput").ap()
             for w in "kvro"}
    vec_names = ["bk", "bv", "br", "bo", "tmk", "tmv", "tmr", "uu", "ww"]
    vecs_in = {n: nc.dram_tensor(n, [128, DC], F32, kind="ExternalInput").ap()
               for n in vec_names}
    oh_in = nc.dram_tensor("oh", [128, 8], F32, kind="ExternalInput").ap()
    ident_in = nc.dram_tensor("ident", [128, 128], F32, kind="ExternalInput").ap()
    gateT_in = nc.dram_tensor("gateT", [DC, 128, 8], F32, kind="ExternalInput").ap()
    sgwT_in = nc.dram_tensor("sgwT", [128, DC], F32, kind="ExternalInput").ap()
    w13T = nc.dram_tensor("w13T", [NEXP, 2 * JC, 128, DC * 128], F32,
                          kind="ExternalInput").ap()
    w2T = nc.dram_tensor("w2T", [NEXP, DC, 128, JC * 128], F32,
                         kind="ExternalInput").ap()
    out_dram = nc.dram_tensor("out", [NT, DC, 128, TILE], F32,
                              kind="ExternalOutput").ap()

    sk = "ExternalOutput" if debug else "Internal"
    a_sp = nc.dram_tensor("a_sp", [NT, DC, 128, TILE], F32, kind=sk).ap()
    b_sp = nc.dram_tensor("b_sp", [NT, DC, 128, TILE], F32, kind=sk).ap()
    p_sp = nc.dram_tensor("p_sp", [NT, DC, 128, TILE], BF16, kind=sk).ap()
    r_sp = nc.dram_tensor("r_sp", [NT, DC, 128, TILE], BF16, kind=sk).ap()
    x1_sp = nc.dram_tensor("x1_sp", [NT, DC, 128, TILE], F32, kind=sk).ap()
    wrow_dram = nc.dram_tensor("wrow_dram", [9, 1024], F32, kind=sk).ap()
    cc_in = nc.dram_tensor("cc_in", [128, 48], F32)
    cc_out = nc.dram_tensor("cc_out", [8 * 128, 48], F32)
    if debug:
        dbg = nc.dram_tensor("dbg", [128, 128], F32, kind="ExternalOutput").ap()
        dbg_xn = nc.dram_tensor("dbg_xn", [128, DC * (TILE + 1)], BF16,
                                kind="ExternalOutput").ap()
        dbg_uk = nc.dram_tensor("dbg_uk", [128, DC * TILE], F32,
                                kind="ExternalOutput").ap()
        dbg_mix = nc.dram_tensor("dbg_mix", [128, DC * TILE], BF16,
                                 kind="ExternalOutput").ap()
        dbg_xc = nc.dram_tensor("dbg_xc", [128, DC], BF16,
                                kind="ExternalOutput").ap()
        dbg_lr = nc.dram_tensor("dbg_lr", [1, 8], F32,
                                kind="ExternalOutput").ap()
        dbg_acc = nc.dram_tensor("dbg_acc", [2, 128, DC * TILE], F32,
                                 kind="ExternalOutput").ap()

    es = ExitStack()

    def sbt(name, p, f, dt):
        return es.enter_context(nc.sbuf_tensor(name, [p, f], dt))

    with es:
        vt = {n: sbt("c_" + n, 128, DC, F32) for n in vec_names}
        ubk = sbt("c_ubk", 128, DC, F32)
        oh = sbt("c_oh", 128, 8, F32)
        ident = sbt("c_ident", 128, 128, F32)
        gate_bf = sbt("c_gate", 128, DC * 8, BF16)
        sgw_bf = sbt("c_sgw", 128, DC, BF16)
        ones_col = sbt("c_onc", 128, 1, F32)
        ones_row = sbt("c_onr", 1, 128, F32)
        xlead = sbt("xlead", 128, DC, F32)
        xn_carry = sbt("xn_carry", 128, DC, BF16)
        carry_a = sbt("carry_a", 128, DC, F32)
        carry_b = sbt("carry_b", 128, DC, F32)
        carry_c = sbt("carry_c", 128, DC, F32)
        a_in = sbt("a_in", 128, DC, F32)
        b_in = sbt("b_in", 128, DC, F32)
        gath = sbt("gath", 128, 8 * 48, F32)
        summ = sbt("summ", 128, 48, F32)
        st8 = sbt("st8", 128, 4 * DC, F32)

        AR1 = sbt("AR1", 128, DC * TILE, F32)
        AR2 = sbt("AR2", 128, DC * TILE, F32)
        AR3 = sbt("AR3", 128, DC * TILE, BF16)
        AR4 = sbt("AR4", 128, DC * TILE, BF16)
        AR5 = sbt("AR5", 128, DC * (TILE + 1), BF16)
        AR6 = sbt("AR6", 128, JC * TILE, BF16)
        wq = [sbt(f"wq{i}", 128, DC * 256, BF16) for i in range(2)]
        w2q = [sbt(f"w2q{i}", 128, JC * 128, BF16) for i in range(3)]
        t_sb = sbt("t_sb", 128, TILE, F32)
        e1a_sb = sbt("e1a_sb", 128, TILE, F32)
        e1_sb = sbt("e1_sb", 128, TILE, F32)
        e2a_sb = sbt("e2a_sb", 128, TILE, F32)
        e2_sb = sbt("e2_sb", 128, TILE, F32)
        e2v_sb = sbt("e2v_sb", 128, TILE, F32)
        a_pp = [sbt(f"a_pp{i}", 128, TILE, F32) for i in range(2)]
        b_pp = [sbt(f"b_pp{i}", 128, TILE, F32) for i in range(2)]
        cum_pp = [sbt(f"cum_pp{i}", 128, TILE, F32) for i in range(2)]
        p_pp = [sbt(f"p_pp{i}", 128, TILE, BF16) for i in range(2)]
        r_pp = [sbt(f"r_pp{i}", 128, TILE, BF16) for i in range(2)]
        rows = sbt("rows", 1, 3 * TILE, F32)
        d2_bf = sbt("d2bf", 128, TILE, BF16)
        bcr_sb = sbt("bcr", 128, TILE, F32)
        bcmr_sb = sbt("bcmr", 128, TILE, F32)
        lead_rows = sbt("lead_rows", 1, 8, F32)
        eps_row = sbt("eps_row", 1, 1, F32)
        lead_bc = sbt("lead_bc", 128, 2, F32)
        probs = sbt("probs", 128, 8, F32)
        pr_sc = sbt("pr_sc", 128, 8, F32)
        rowsc = sbt("rowsc", 128, 2, F32)
        sgl = sbt("sgl", 128, 1, F32)
        stage_row = sbt("stage_row", 1, 9 * 128, F32)
        wrow1 = sbt("wrow1", 1, 1024, F32)
        wtil = cum_pp[0][:].bitcast(BF16)

        sqp = [t_sb, e1a_sb]

        class Sem:
            def __init__(self, name):
                self.h = es.enter_context(nc.semaphore(name))
                self.n = 0

        s_cn = Sem("s_cn")
        s_x = Sem("s_x")
        s_w = [Sem("s_w0"), Sem("s_w1")]
        s_w2 = [Sem("s_w20"), Sem("s_w21"), Sem("s_w22")]
        s_sp = [Sem("s_sp0"), Sem("s_sp1")]
        s_ld = [Sem("s_ld0"), Sem("s_ld1")]
        s_pe = Sem("s_pe")
        s_dve = Sem("s_dve")
        s_act = Sem("s_act")
        s_cc = Sem("s_cc")
        s_row = Sem("s_row")
        s_out = Sem("s_out")

        P = Prog()

        def ev(eng, fn):
            P.ops[eng].append(fn)

        def wt(eng, sem, n):
            if n > 0:
                P.ops[eng].append(lambda e, s=sem.h, v=n: e.wait_ge(s, v))

        def inc(eng, sem, fn, k=1):
            sem.n += k
            P.ops[eng].append(lambda e, f=fn, s=sem.h, kk=k: f(e).then_inc(s, kk))
            return sem.n


        def cs(buf, c, w=TILE):
            return buf[:, c * w:(c + 1) * w]

        def xn3():
            return AR5[:].rearrange("p (c j) -> p c j", c=DC)

        for n in vec_names:
            inc("sync", s_cn, lambda e, d=vt[n], s=vecs_in[n]:
                e.dma_start(out=d[:], in_=s), 16)
        inc("sync", s_cn, lambda e: e.dma_start(out=oh[:], in_=oh_in), 16)
        inc("sync", s_cn, lambda e: e.dma_start(out=ident[:], in_=ident_in), 16)
        inc("sync", s_cn, lambda e: e.dma_start(
            out=xlead[:], in_=xlead_in), 16)
        inc("gpsimd", s_cn, lambda e: e.dma_start(
            out=gate_bf[:].rearrange("p (c e) -> p c e", c=DC),
            in_=gateT_in.rearrange("c p e -> p c e")), 16)
        inc("gpsimd", s_cn, lambda e: e.dma_start(
            out=sgw_bf[:], in_=sgwT_in), 16)
        cn_done = s_cn.n
        wt("vector", s_cn, cn_done)
        ev("vector", lambda e: e.tensor_add(ubk[:], vt["uu"][:], vt["bk"][:]))
        ev("vector", lambda e: e.memset(ones_col[:], 1.0))
        ev("vector", lambda e: e.memset(ones_row[:], 1.0))
        ev("vector", lambda e: e.memset(eps_row[:], EPS))
        ev("vector", lambda e: e.memset(carry_a[:], 0.0))
        ev("vector", lambda e: e.memset(carry_b[:], 0.0))
        m_init = inc("vector", s_dve, lambda e: e.memset(carry_c[:], 0.0))
        wt("scalar", s_cn, cn_done)
        m_sq = inc("scalar", s_act, lambda e: e.activation(
            sqp[0][:, 0:DC], xlead[:], AF.Square))
        wt("tensor", s_dve, m_init)
        wt("tensor", s_act, m_sq)
        ps_s1 = es.enter_context(nc.psum_tensor("ps_s1", [1, TILE], F32))
        ps_s2 = es.enter_context(nc.psum_tensor("ps_s2", [1, TILE], F32))
        psb = [es.enter_context(nc.psum_tensor(f"psb{i}", [128, 512], F32))
               for i in range(6)]
        m_lead = inc("tensor", s_pe, lambda e: e.matmul(
            ps_s1[:, 0:DC], ones_col[:], xlead[:], start=True, stop=True))
        m_lead = inc("tensor", s_pe, lambda e: e.matmul(
            ps_s2[:, 0:DC], ones_col[:], sqp[0][:, 0:DC], start=True, stop=True))
        wt("vector", s_pe, m_lead)
        hs = inc("vector", s_dve, lambda e: e.reduce_sum(
            lead_rows[:, 0:1], ps_s1[:, 0:DC], axis=AX.X))
        hs = inc("vector", s_dve, lambda e: e.reduce_sum(
            lead_rows[:, 1:2], ps_s2[:, 0:DC], axis=AX.X))
        wt("vector", s_dve, hs)
        ev("vector", lambda e: e.tensor_scalar(
            lead_rows[:, 2:3], lead_rows[:, 0:1], 1.0 / D, None, ALU.mult))
        hs = inc("vector", s_dve, lambda e: e.tensor_scalar(
            lead_rows[:, 3:4], lead_rows[:, 1:2], 1.0 / D, None, ALU.mult))
        wt("vector", s_dve, hs)
        hs = inc("vector", s_dve, lambda e: e.tensor_mul(
            lead_rows[:, 4:5], lead_rows[:, 2:3], lead_rows[:, 2:3]))
        wt("vector", s_dve, hs)
        hs = inc("vector", s_dve, lambda e: e.tensor_sub(
            lead_rows[:, 4:5], lead_rows[:, 3:4], lead_rows[:, 4:5]))
        wt("vector", s_dve, hs)
        m_var = inc("vector", s_dve, lambda e: e.tensor_scalar(
            lead_rows[:, 4:5], lead_rows[:, 4:5], 0.0, None, ALU.max))
        wt("scalar", s_dve, m_var)
        hsa = inc("scalar", s_act, lambda e: e.activation(
            lead_rows[:, 5:6], lead_rows[:, 4:5], AF.Ln, bias=eps_row[0:1, 0:1]))
        wt("scalar", s_act, hsa)
        m_rst = inc("scalar", s_act, lambda e: e.activation(
            lead_rows[:, 5:6], lead_rows[:, 5:6], AF.Exp, scale=-0.5))
        wt("vector", s_act, m_rst)
        m_mr = inc("vector", s_dve, lambda e: e.tensor_mul(
            lead_rows[:, 6:7], lead_rows[:, 2:3], lead_rows[:, 5:6]))
        wt("tensor", s_dve, m_mr)
        m_bc = inc("tensor", s_pe, lambda e: e.matmul(
            psb[0][:, 0:2], ones_row[:], lead_rows[:, 5:7], start=True, stop=True))
        wt("vector", s_pe, m_bc)
        m_cp = inc("vector", s_dve, lambda e: e.tensor_copy(lead_bc[:],
                                                            psb[0][:, 0:2]))
        wt("vector", s_dve, m_cp)
        m_xc = inc("vector", s_dve, lambda e: e.tensor_scalar(
            xn_carry[:], xlead[:], lead_bc[:, 0:1], lead_bc[:, 1:2],
            ALU.mult, ALU.subtract))
        if debug:
            wt("sync", s_dve, m_xc)
            inc("sync", s_out, lambda e: e.dma_start(
                out=dbg_xc, in_=xn_carry[:]), 16)
            inc("sync", s_out, lambda e: e.dma_start(
                out=dbg_lr, in_=lead_rows[:]), 16)
        psb0_free = m_cp

        ln_state = {"sq_guard": [0, 0], "bc_guard": 0, "row_guard": 0,
                    "stat_guard": 0, "bcr_free": 0}

        def ln_stats_chunk(buf, c, sqb=None, sq_eng="scalar",
                           ready_act=(), ready_pe=()):
            sq = sqp[c % 2] if sqb is None else sqb[c % 2]
            wt(sq_eng, s_pe, ln_state["sq_guard"][c % 2])
            for sm, mk in ready_act:
                wt(sq_eng, sm, mk)
            if sq_eng == "scalar":
                sqm = inc(sq_eng, s_act, lambda e, c=c, sq=sq: e.activation(
                    sq[:], cs(buf, c), AF.Square))
            else:
                sqm = inc(sq_eng, s_act, lambda e, c=c, sq=sq: e.tensor_mul(
                    sq[:], cs(buf, c), cs(buf, c)))
            for sm, mk in ready_pe:
                wt("tensor", sm, mk)
            if c == 0:
                wt("tensor", s_dve, ln_state["row_guard"])
            ev("tensor", lambda e, c=c: e.matmul(
                ps_s1[:], ones_col[:], cs(buf, c),
                start=(c == 0), stop=(c == DC - 1)))
            wt("tensor", s_act, sqm)
            pm = inc("tensor", s_pe, lambda e, c=c, sq=sq: e.matmul(
                ps_s2[:], ones_col[:], sq[:],
                start=(c == 0), stop=(c == DC - 1)))
            ln_state["sq_guard"][c % 2] = pm
            return pm

        def emit_ln(buf, ready_pe, ready_act):
            ra = [ready_act] if isinstance(ready_act, tuple) else ready_act
            rp = [ready_pe] if isinstance(ready_pe, tuple) else ready_pe
            for c in range(DC):
                stat_mark = ln_stats_chunk(buf, c,
                                           ready_act=(ra if c == 0 else ()),
                                           ready_pe=(rp if c == 0 else ()))
            return ln_bc(ln_rows(stat_mark))

        def ln_finish(stat_mark, rows_eng="vector"):
            return ln_bc(ln_rows(stat_mark, rows_eng))

        def ln_rows(stat_mark, eng="vector"):
            wt("vector", s_pe, stat_mark)
            ev("vector", lambda e: e.tensor_scalar(
                rows[:, 0:TILE], ps_s1[:], 1.0 / D, None, ALU.mult))
            pr = inc("vector", s_dve, lambda e: e.tensor_scalar(
                rows[:, TILE:2 * TILE], ps_s2[:], 1.0 / D, None, ALU.mult))
            if eng != "vector":
                wt(eng, s_dve, pr)
            ev(eng, lambda e: e.tensor_mul(
                rows[:, 2 * TILE:3 * TILE], rows[:, 0:TILE], rows[:, 0:TILE]))
            ev(eng, lambda e: e.tensor_sub(
                rows[:, TILE:2 * TILE], rows[:, TILE:2 * TILE],
                rows[:, 2 * TILE:3 * TILE]))
            row_mark = inc(eng, s_dve, lambda e: e.tensor_scalar(
                rows[:, TILE:2 * TILE], rows[:, TILE:2 * TILE], 0.0, None,
                ALU.max))
            ln_state["row_guard"] = row_mark
            wt("scalar", s_dve, row_mark)
            ev("scalar", lambda e: e.activation(
                rows[:, TILE:2 * TILE], rows[:, TILE:2 * TILE], AF.Ln, bias=eps_row[0:1, 0:1]))
            rst_mark = inc("scalar", s_act, lambda e: e.activation(
                rows[:, TILE:2 * TILE], rows[:, TILE:2 * TILE], AF.Exp,
                scale=-0.5))
            wt(eng, s_act, rst_mark)
            mr_mark = inc(eng, s_dve, lambda e: e.tensor_mul(
                rows[:, 2 * TILE:3 * TILE], rows[:, 0:TILE],
                rows[:, TILE:2 * TILE]))
            return mr_mark

        def ln_bc(mr_mark):
            wt("tensor", s_dve, mr_mark)
            wt("tensor", s_act, ln_state["bc_guard"])
            ev("tensor", lambda e: e.matmul(
                psb[4][:], ones_row[:], rows[:, TILE:2 * TILE],
                start=True, stop=True))
            bcm = inc("tensor", s_pe, lambda e: e.matmul(
                psb[5][:], ones_row[:], rows[:, 2 * TILE:3 * TILE],
                start=True, stop=True))
            wt("scalar", s_pe, bcm)
            wt("scalar", s_dve, ln_state["bcr_free"])
            ev("scalar", lambda e: e.copy(bcr_sb[:], psb[4][:]))
            bc_mark = inc("scalar", s_act, lambda e: e.copy(bcmr_sb[:], psb[5][:]))
            ln_state["bc_guard"] = bc_mark
            return bc_mark

        wq_ready = {}
        npiece = [0]

        def stream_attn_piece(w, q, gate):
            i = npiece[0]
            par = i % 2
            npiece[0] += 1
            wt("gpsimd", s_pe, gate)
            m = inc("gpsimd", s_w[par], lambda e, w=w, q=q, par=par:
                    e.dma_start(out=wq[par][:], in_=wT_in[w][q]), 16)
            wq_ready[i] = (par, m)
            return i

        xbuf_guard = 0
        uk_marks = [0] * DC
        e2v_marks = [0] * DC
        uk_guard = [0] * DC
        spill_guard = [0, 0]
        rspill = [0, 0]
        wq_gate = [0, 0]
        carry_mark = [m_xc]
        carry_mark_g = [m_init]
        d_bf = e2a_sb[:].bitcast(BF16)[:, 0:TILE]
        prev_wr_pm = 0
        silu_last = 0
        mixk_mark = [0] * NT
        mixv_mark = [0] * NT
        x_marks = [0] * NT

        def emit_xn_mixes(g, bc_mark):
            nonlocal_state = None
            wt("vector", s_act, bc_mark)
            ev("vector", lambda e: e.tensor_copy(
                xn3()[:, :, 0:1], xn_carry[:].rearrange("p (c o) -> p c o", o=1)))
            for c in range(DC):
                ev("vector", lambda e, c=c: e.tensor_mul(
                    t_sb[:], cs(AR1, c), bcr_sb[:]))
                ev("vector", lambda e, c=c: e.tensor_sub(
                    xn3()[:, c, 1:TILE + 1], t_sb[:], bcmr_sb[:]))
            xg = inc("vector", s_dve, lambda e: e.tensor_copy(
                xn_carry[:].rearrange("p (c o) -> p c o", o=1),
                xn3()[:, :, TILE:TILE + 1]))
            ln_state["bcr_free"] = xg
            if g + 1 < NT:
                wt("sync", s_dve, xg)
                x_marks[g + 1] = inc("sync", s_x, lambda e, g=g: e.dma_start(
                    out=AR1[:], in_=xT[g + 1]), 16)
            wt("vector", s_act, silu_last)
            for c in range(DC):
                ev("vector", lambda e, c=c, d=d_bf: e.tensor_sub(
                    d, xn3()[:, c, 1:TILE + 1], xn3()[:, c, 0:TILE]))
                fn = lambda e, c=c, d=d_bf: e.scalar_tensor_tensor(
                    cs(AR3, c), d, vt["tmk"][:, c:c + 1],
                    xn3()[:, c, 0:TILE], ALU.mult, ALU.add)
                if c == DC - 1:
                    mixk_mark[g] = inc("vector", s_dve, fn)
                else:
                    ev("vector", fn)
            wt("vector", s_pe, prev_wr_pm)
            for c in range(DC):
                ev("vector", lambda e, c=c, d=d_bf: e.tensor_sub(
                    d, xn3()[:, c, 1:TILE + 1], xn3()[:, c, 0:TILE]))
                fn = lambda e, c=c, d=d_bf: e.scalar_tensor_tensor(
                    cs(AR4, c), d, vt["tmv"][:, c:c + 1],
                    xn3()[:, c, 0:TILE], ALU.mult, ALU.add)
                if c == DC - 1:
                    mixv_mark[g] = inc("vector", s_dve, fn)
                else:
                    ev("vector", fn)
            return xg

        x_marks[0] = inc("sync", s_x, lambda e: e.dma_start(
            out=AR1[:], in_=xT[0]), 16)
        bc0 = emit_ln(AR1, (s_x, x_marks[0]), (s_x, x_marks[0]))
        xbuf_guard = emit_xn_mixes(0, bc0)

        for g in range(NT):
            have_next = g + 1 < NT
            pieces = [stream_attn_piece("k", 0, wq_gate[0]),
                      stream_attn_piece("k", 1, wq_gate[1])]
            wt("tensor", s_dve, mixk_mark[g])
            for c in range(DC):
                par, wm = wq_ready[pieces[c // 2]]
                wt("tensor", s_w[par], wm)
                wt("tensor", s_act, uk_marks[(c - 2) % DC])
                for kc in range(DC):
                    mm = lambda e, c=c, kc=kc, par=par: e.matmul(
                        psb[c % 2][:],
                        wq[par][:, kc * 256 + (c % 2) * 128:
                                kc * 256 + (c % 2) * 128 + 128],
                        cs(AR3, kc), start=(kc == 0), stop=(kc == DC - 1))
                    if kc == DC - 1:
                        pm = inc("tensor", s_pe, mm)
                    else:
                        ev("tensor", mm)
                if c % 2 == 1 and c // 2 + 2 < 8:
                    pieces.append(stream_attn_piece("k", c // 2 + 2, pm))
                wt("scalar", s_pe, pm)
                wt("scalar", s_dve, uk_guard[c])
                uk_marks[c] = inc("scalar", s_act, lambda e, c=c: e.add(
                    cs(AR2, c), psb[c % 2][:], ubk[:, c:c + 1]))
            wq_gate = [pm, pm]
            wk_pm = pm
            pieces = [stream_attn_piece("v", 0, wq_gate[0]),
                      stream_attn_piece("v", 1, wq_gate[1])]
            wt("tensor", s_dve, mixv_mark[g])
            for c in range(DC):
                par, wm = wq_ready[pieces[c // 2]]
                wt("tensor", s_w[par], wm)
                wt("tensor", s_dve, e2v_marks[(c - 2) % DC])
                for kc in range(DC):
                    mm = lambda e, c=c, kc=kc, par=par: e.matmul(
                        psb[2 + c % 2][:],
                        wq[par][:, kc * 256 + (c % 2) * 128:
                                kc * 256 + (c % 2) * 128 + 128],
                        cs(AR4, kc), start=(kc == 0), stop=(kc == DC - 1))
                    if kc == DC - 1:
                        pm = inc("tensor", s_pe, mm)
                    else:
                        ev("tensor", mm)
                if c % 2 == 1 and c // 2 + 2 < 8:
                    pieces.append(stream_attn_piece("v", c // 2 + 2, pm))
                if have_next and 1 <= c <= 8:
                    for sc in (2 * (c - 1), 2 * (c - 1) + 1):
                        stat_next = ln_stats_chunk(
                            AR1, sc, sqb=[bcr_sb, bcmr_sb],
                            ready_pe=([(s_x, x_marks[g + 1])]
                                      if sc == 0 else ()),
                            ready_act=([(s_x, x_marks[g + 1]),
                                        (s_dve, xbuf_guard)]
                                       if sc == 0 else ()))
                if have_next and c == 9:
                    mr_next = ln_rows(stat_next)
                if have_next and c == 12:
                    bc_next = ln_bc(mr_next)
                wt("vector", s_act, uk_marks[c])
                ev("vector", lambda e, c=c: e.tensor_scalar(
                    t_sb[:], cs(AR2, c), vt["ww"][:, c:c + 1], None, ALU.max))
                ev("vector", lambda e, c=c: e.tensor_scalar(
                    e1a_sb[:], t_sb[:], vt["ww"][:, c:c + 1], -1.0,
                    ALU.add, ALU.mult))
                pr_mark = inc("vector", s_dve, lambda e, c=c: e.tensor_sub(
                    e2a_sb[:], cs(AR2, c), t_sb[:]))
                uk_guard[c] = pr_mark
                wt("scalar", s_dve, pr_mark)
                ev("scalar", lambda e: e.activation(e1_sb[:], e1a_sb[:], AF.Exp))
                ex_mark = inc("scalar", s_act, lambda e: e.activation(
                    e2_sb[:], e2a_sb[:], AF.Exp))
                wt("vector", s_pe, pm)
                wt("vector", s_act, ex_mark)
                e2v_marks[c] = inc("vector", s_dve, lambda e, c=c:
                                   e.scalar_tensor_tensor(
                                       e2v_sb[:], psb[2 + c % 2][:],
                                       vt["bv"][:, c:c + 1], e2_sb[:],
                                       ALU.add, ALU.mult))
                par2 = c % 2
                wt("vector", s_sp[par2], spill_guard[par2])
                wt("vector", s_dve, carry_mark[0])
                ev("vector", lambda e, c=c, p2=par2: e.tensor_tensor_scan(
                    a_pp[p2][:], e1_sb[:], e2v_sb[:], carry_a[:, c:c + 1],
                    ALU.mult, ALU.add))
                ev("vector", lambda e, c=c, p2=par2: e.tensor_tensor_scan(
                    b_pp[p2][:], e1_sb[:], e2_sb[:], carry_b[:, c:c + 1],
                    ALU.mult, ALU.add))
                ev("vector", lambda e, c=c, p2=par2: e.tensor_tensor_scan(
                    cum_pp[p2][:], e1a_sb[:], e1a_sb[:], carry_c[:, c:c + 1],
                    ALU.add, ALU.bypass))
                ev("vector", lambda e, c=c, p2=par2: e.tensor_copy(
                    carry_a[:, c:c + 1], a_pp[p2][:, TILE - 1:TILE]))
                ev("vector", lambda e, c=c, p2=par2: e.tensor_copy(
                    carry_b[:, c:c + 1], b_pp[p2][:, TILE - 1:TILE]))
                cm = inc("vector", s_dve, lambda e, c=c, p2=par2: e.tensor_copy(
                    carry_c[:, c:c + 1], cum_pp[p2][:, TILE - 1:TILE]))
                carry_mark[0] = cm
                carry_mark_g[0] = cm
                if c == 0:
                    wt("vector", s_pe, wk_pm)
                ev("vector", lambda e, c=c: e.tensor_sub(
                    d2_bf[:], xn3()[:, c, 1:TILE + 1], xn3()[:, c, 0:TILE]))
                mr_c = inc("vector", s_dve, lambda e, c=c:
                           e.scalar_tensor_tensor(
                               cs(AR3, c), d2_bf[:], vt["tmr"][:, c:c + 1],
                               xn3()[:, c, 0:TILE], ALU.mult, ALU.add))
                wt("scalar", s_dve, cm)
                wt("scalar", s_sp[par2], spill_guard[par2])
                pexp = inc("scalar", s_act, lambda e, p2=par2: e.activation(
                    p_pp[p2][:], cum_pp[p2][:], AF.Exp))
                if c < JC:
                    if c == 0:
                        wt("scalar", s_pe, prev_wr_pm)
                    wt("scalar", s_dve, mr_c)
                    sl = inc("scalar", s_act, lambda e, c=c: e.activation(
                        cs(AR6, c), cs(AR3, c), AF.Silu))
                wt("sync", s_dve, cm)
                wt("sync", s_act, pexp)
                inc("sync", s_sp[par2], lambda e, g=g, c=c, p2=par2:
                    e.dma_start(out=a_sp[g, c], in_=a_pp[p2][:]), 16)
                inc("sync", s_sp[par2], lambda e, g=g, c=c, p2=par2:
                    e.dma_start(out=b_sp[g, c], in_=b_pp[p2][:]), 16)
                sm = inc("sync", s_sp[par2], lambda e, g=g, c=c, p2=par2:
                         e.dma_start(out=p_sp[g, c], in_=p_pp[p2][:]), 16)
                spill_guard[par2] = sm
            wq_gate = [pm, pm]
            wv_pm = pm
            mixr_last = mr_c
            wt("scalar", s_pe, wv_pm)
            wt("scalar", s_dve, mixr_last)
            for c in range(JC, DC):
                sl = inc("scalar", s_act, lambda e, c=c: e.activation(
                    cs(AR4, c), cs(AR3, c), AF.Silu))
            silu_last = sl
            pieces = [stream_attn_piece("r", 0, wq_gate[0]),
                      stream_attn_piece("r", 1, wq_gate[1])]
            wt("tensor", s_act, silu_last)
            for c in range(DC):
                par, wm = wq_ready[pieces[c // 2]]
                wt("tensor", s_w[par], wm)
                wt("tensor", s_act, uk_marks[(c - 2) % DC])
                for kc in range(DC):
                    mov = (cs(AR6, kc) if kc < JC else cs(AR4, kc))
                    mm = lambda e, c=c, kc=kc, par=par, mov=mov: e.matmul(
                        psb[c % 2][:],
                        wq[par][:, kc * 256 + (c % 2) * 128:
                                kc * 256 + (c % 2) * 128 + 128],
                        mov, start=(kc == 0), stop=(kc == DC - 1))
                    if kc == DC - 1:
                        pm = inc("tensor", s_pe, mm)
                    else:
                        ev("tensor", mm)
                if c % 2 == 1 and c // 2 + 2 < 8:
                    pieces.append(stream_attn_piece("r", c // 2 + 2, pm))
                wt("scalar", s_pe, pm)
                wt("scalar", s_sp[c % 2], rspill[c % 2])
                uk_marks[c] = em = inc("scalar", s_act, lambda e, c=c: e.add(
                    r_pp[c % 2][:], psb[c % 2][:], vt["br"][:, c:c + 1]))
                wt("sync", s_act, em)
                rm = inc("sync", s_sp[c % 2], lambda e, g=g, c=c:
                         e.dma_start(out=r_sp[g, c], in_=r_pp[c % 2][:]), 16)
                rspill[c % 2] = rm
            wq_gate = [pm, pm]
            prev_wr_pm = pm
            if have_next:
                xbuf_guard = emit_xn_mixes(g + 1, bc_next)
        mix_guard = prev_wr_pm

        P.run(nc)
        P = Prog()

        wt("scalar", s_dve, max(carry_mark[0], carry_mark_g[0]))
        sA = inc("scalar", s_act, lambda e: e.activation(
            summ[:, 0:DC], carry_c[:], AF.Exp))
        ev("vector", lambda e: e.tensor_copy(summ[:, DC:2 * DC], carry_a[:]))
        sB = inc("vector", s_dve, lambda e: e.tensor_copy(
            summ[:, 2 * DC:3 * DC], carry_b[:]))
        wt("sync", s_act, sA)
        wt("sync", s_dve, sB)
        scc = inc("sync", s_x, lambda e: e.dma_start(out=cc_in.ap(), in_=summ[:]),
                  16)
        wt("gpsimd", s_x, scc)
        ccm = inc("gpsimd", s_cc, lambda e: e.collective_compute(
            "AllGather", ALU.bypass, replica_groups=[list(range(8))],
            ins=[cc_in.ap().opt()], outs=[cc_out.ap().opt()]))
        wt("gpsimd", s_cc, ccm)
        gm = inc("gpsimd", s_cc, lambda e: e.dma_start(
            out=gath[:].rearrange("p (r c) -> p r c", r=8),
            in_=cc_out.ap().rearrange("(r p) c -> p r c", p=128)), 16)
        wt("vector", s_cc, gm)
        for r in range(8):
            if r % 4 == 0:
                ev("vector", lambda e: e.memset(st8[:, 0:2 * DC], 0.0))
            if r == 0:
                ev("vector", lambda e: e.memset(st8[:, 2 * DC:4 * DC], 0.0))
            ev("vector", lambda e, r=r: e.scalar_tensor_tensor(
                st8[:, 2 * DC:3 * DC], st8[:, 0:DC], oh[:, r:r + 1],
                st8[:, 2 * DC:3 * DC], ALU.mult, ALU.add))
            ev("vector", lambda e, r=r: e.scalar_tensor_tensor(
                st8[:, 3 * DC:4 * DC], st8[:, DC:2 * DC], oh[:, r:r + 1],
                st8[:, 3 * DC:4 * DC], ALU.mult, ALU.add))
            if r < 7:
                g0 = 48 * r
                ev("vector", lambda e, g0=g0: e.tensor_mul(
                    st8[:, 0:DC], st8[:, 0:DC], gath[:, g0:g0 + DC]))
                ev("vector", lambda e, g0=g0: e.tensor_add(
                    st8[:, 0:DC], st8[:, 0:DC], gath[:, g0 + DC:g0 + 2 * DC]))
                ev("vector", lambda e, g0=g0: e.tensor_mul(
                    st8[:, DC:2 * DC], st8[:, DC:2 * DC], gath[:, g0:g0 + DC]))
                ev("vector", lambda e, g0=g0: e.tensor_add(
                    st8[:, DC:2 * DC], st8[:, DC:2 * DC],
                    gath[:, g0 + 2 * DC:g0 + 3 * DC]))
        ev("vector", lambda e: e.tensor_copy(a_in[:], st8[:, 2 * DC:3 * DC]))
        comb_done = inc("vector", s_dve, lambda e: e.tensor_copy(
            b_in[:], st8[:, 3 * DC:4 * DC]))
        P.run(nc)
        P = Prog()

        wt("sync", s_sp[0], s_sp[0].n)
        wt("sync", s_sp[1], s_sp[1].n)
        ld_guard = [0, 0]
        wrbuf = [AR3, AR4]
        wr_pair_guard = [mix_guard, mix_guard]
        att_evac_guard = [0] * DC
        x1_done = [None] * NT
        wr_marks = [0] * NT
        cumsq = [cum_pp[0], cum_pp[1]]

        def emit_corr(g):
            wb = wrbuf[g % 2]
            if g == 0:
                wt("vector", s_dve, comb_done)
            wt("vector", s_pe, wr_pair_guard[g % 2])
            for c in range(DC):
                par = c % 2
                wt("sync", s_dve, ld_guard[par])
                inc("sync", s_ld[par], lambda e, g=g, c=c, par=par:
                    e.dma_start(out=a_pp[par][:], in_=a_sp[g, c]), 16)
                inc("sync", s_ld[par], lambda e, g=g, c=c, par=par:
                    e.dma_start(out=b_pp[par][:], in_=b_sp[g, c]), 16)
                inc("sync", s_ld[par], lambda e, g=g, c=c, par=par:
                    e.dma_start(out=p_pp[par][:], in_=p_sp[g, c]), 16)
                lm = inc("sync", s_ld[par], lambda e, g=g, c=c, par=par:
                         e.dma_start(out=r_pp[par][:], in_=r_sp[g, c]), 16)
                wt("vector", s_ld[par], lm)
                ev("vector", lambda e, c=c, par=par: e.scalar_tensor_tensor(
                    t_sb[:], p_pp[par][:], a_in[:, c:c + 1], a_pp[par][:],
                    ALU.mult, ALU.add))
                ev("vector", lambda e, c=c, par=par:
                   e.scalar_tensor_tensor(
                       e1a_sb[:], p_pp[par][:], b_in[:, c:c + 1],
                       b_pp[par][:], ALU.mult, ALU.add))
                ev("vector", lambda e: e.reciprocal(e2a_sb[:], e1a_sb[:]))
                ev("vector", lambda e: e.tensor_mul(e2v_sb[:], t_sb[:],
                                                    e2a_sb[:]))
                lg = inc("vector", s_dve, lambda e, c=c, par=par, wb=wb:
                         e.tensor_mul(cs(wb, c), e2v_sb[:], r_pp[par][:]))
                ld_guard[par] = lg
            wr_marks[g] = lg

        emit_corr(0)
        for g in range(NT):
            wt("sync", s_dve, xbuf_guard)
            x_mark = inc("sync", s_x, lambda e, g=g: e.dma_start(
                out=AR1[:], in_=xT[g]), 16)
            wb = wrbuf[g % 2]
            pieces = [stream_attn_piece("o", 0, wq_gate[0]),
                      stream_attn_piece("o", 1, wq_gate[1])]
            wt("tensor", s_dve, wr_marks[g])
            for c in range(DC):
                par, wm = wq_ready[pieces[c // 2]]
                wt("tensor", s_w[par], wm)
                wt("tensor", s_act, uk_marks[(c - 2) % DC])
                for kc in range(DC):
                    mm = lambda e, c=c, kc=kc, par=par, wb=wb: e.matmul(
                        psb[c % 2][:],
                        wq[par][:, kc * 256 + (c % 2) * 128:
                                kc * 256 + (c % 2) * 128 + 128],
                        cs(wb, kc), start=(kc == 0), stop=(kc == DC - 1))
                    if kc == DC - 1:
                        pm = inc("tensor", s_pe, mm)
                    else:
                        ev("tensor", mm)
                if c % 2 == 1 and c // 2 + 2 < 8:
                    pieces.append(stream_attn_piece("o", c // 2 + 2, pm))
                wt("scalar", s_pe, pm)
                wt("scalar", s_sp[c % 2], att_evac_guard[c])
                uk_marks[c] = inc("scalar", s_act, lambda e, c=c: e.add(
                    cs(AR2, c), psb[c % 2][:], vt["bo"][:, c:c + 1]))
                if c >= 2:
                    ln_stats_chunk(AR2, c - 2, sqb=cumsq,
                                   ready_pe=[(s_act, uk_marks[c - 2])])
            wq_gate = [pm, pm]
            wr_pair_guard[g % 2] = pm
            for sc in (DC - 2, DC - 1):
                stat_mark = ln_stats_chunk(AR2, sc, sqb=cumsq,
                                           ready_pe=[(s_act, uk_marks[sc])])
            if g + 1 < NT:
                emit_corr(g + 1)
            bc2 = ln_finish(stat_mark)
            wt("vector", s_act, bc2)
            wt("vector", s_x, x_mark)
            for c in range(DC):
                ev("vector", lambda e, c=c: e.tensor_mul(
                    t_sb[:], cs(AR2, c), bcr_sb[:]))
                ev("vector", lambda e, c=c: e.tensor_sub(
                    t_sb[:], t_sb[:], bcmr_sb[:]))
                xm = inc("vector", s_dve, lambda e, c=c: e.tensor_add(
                    cs(AR2, c), t_sb[:], cs(AR1, c)))
                wt("scalar", s_dve, xm)
                sm = inc("scalar", s_sp[c % 2], lambda e, g=g, c=c:
                         e.dma_start(out=x1_sp[g, c], in_=cs(AR2, c)), 16)
                att_evac_guard[c] = sm
            xbuf_guard = xm
            ln_state["bcr_free"] = xm
            x1_done[g] = (s_sp[0].n, s_sp[1].n)
        wr_guard = max(wr_pair_guard)

        P.run(nc)
        P = Prog()

        ytg = [AR3, AR4]
        acc = [AR1, AR2]
        mid = [AR6, AR5]
        y_ready = [0, 0]
        acc_zero = [0, 0]
        acc_free_marks = [[0] * DC, [0] * DC]
        w13_ready = {}
        w13_free = [s_pe.n, s_pe.n]
        w13_pc = [0]
        w2_ready = {}
        w2_free = [0, 0, 0]
        w2_pc = [0]
        row_stage_guard = 0
        wrow_guard = 0
        out_guard = [0, 0]
        silu_guard = [0, 0]
        silu_bf = [p_pp[0], p_pp[1]]
        tmp_bf = [r_pp[0], r_pp[1]]
        ps_guard_g = [0, 0]
        ps_guard_u = [0, 0]
        accg = [[0, 0], [0, 0]]
        rt_ps_guard = 0
        tr_ps_guard = 0
        tr_pe_guard = 0

        for h in range(2):
            wt("sync", s_sp[0], s_sp[0].n)
            wt("sync", s_sp[1], s_sp[1].n)
            if h == 0:
                wt("sync", s_pe, wr_guard)
                wt("tensor", s_dve, s_dve.n)
                wt("tensor", s_act, s_act.n)
                wt("scalar", s_sp[0], s_sp[0].n)
                wt("scalar", s_sp[1], s_sp[1].n)
            else:
                wt("tensor", s_act, max(ps_guard_g))
                wt("tensor", s_act, ln_state["bc_guard"])
                wt("tensor", s_dve, max(accg[0] + accg[1]))
            ytg_free = wr_guard if h == 0 else max(w13_free)
            for tg in range(2):
                g = 2 * h + tg
                if h == 0:
                    wt("sync", s_sp[0], x1_done[NT - 1][0])
                    wt("sync", s_sp[1], x1_done[NT - 1][1])
                    wt("sync", s_dve, xbuf_guard)
                    x1ld = inc("sync", s_x, lambda e, g=g, tg=tg: e.dma_start(
                        out=acc[tg][:].rearrange("p (c j) -> p c j", c=DC),
                        in_=x1_sp[g].rearrange("c p j -> p c j")), 16)
                else:
                    wt("sync", s_sp[0], x1_done[g][0])
                    wt("sync", s_sp[1], x1_done[g][1])
                    wt("sync", s_dve, max(ld_guard))
                    x1ld = inc("sync", s_x, lambda e, g=g, tg=tg: e.dma_start(
                        out=acc[tg][:].rearrange("p (c j) -> p c j", c=DC),
                        in_=x1_sp[g].rearrange("c p j -> p c j")), 16)
                act_rdy = [(s_x, x1ld)]
                if tg == 1:
                    act_rdy.append((s_dve, y_ready[0]))
                else:
                    act_rdy.append((s_dve, max(ld_guard[0], ld_guard[1],
                                               xbuf_guard)))
                bc3 = emit_ln(acc[tg], (s_x, x1ld), act_rdy)
                wt("vector", s_act, bc3)
                wt("vector", s_pe, ytg_free)
                for c in range(DC):
                    ev("vector", lambda e, c=c, tg=tg: e.tensor_mul(
                        t_sb[:], cs(acc[tg], c), bcr_sb[:]))
                    fn = lambda e, c=c, tg=tg: e.tensor_sub(
                        cs(ytg[tg], c), t_sb[:], bcmr_sb[:])
                    if c == DC - 1:
                        ym = inc("vector", s_dve, fn)
                    else:
                        ev("vector", fn)
                ln_state["bcr_free"] = ym
                y_ready[tg] = ym
                acc_zero[tg] = inc("vector", s_dve, lambda e, tg=tg: e.memset(
                    acc[tg][:], 0.0))
            for grp in range(8):
                tg, j0 = grp // 4, (grp % 4) * 128
                wt("tensor", s_dve, y_ready[tg])
                wt("tensor", s_dve, rt_ps_guard)
                for kc in range(DC):
                    mm = lambda e, tg=tg, j0=j0, kc=kc: e.matmul(
                        psb[0][:, 0:8],
                        ytg[tg][:, kc * TILE + j0:kc * TILE + j0 + 128],
                        gate_bf[:, kc * 8:(kc + 1) * 8],
                        start=(kc == 0), stop=(kc == DC - 1))
                    if kc == DC - 1:
                        inc("tensor", s_pe, mm)
                    else:
                        ev("tensor", mm)
                for kc in range(DC):
                    mm = lambda e, tg=tg, j0=j0, kc=kc: e.matmul(
                        psb[1][:, 0:1],
                        ytg[tg][:, kc * TILE + j0:kc * TILE + j0 + 128],
                        sgw_bf[:, kc:kc + 1],
                        start=(kc == 0), stop=(kc == DC - 1))
                    if kc == DC - 1:
                        lg_m = inc("tensor", s_pe, mm)
                    else:
                        ev("tensor", mm)
                wt("vector", s_pe, lg_m)
                wt("vector", s_pe, tr_pe_guard)
                wt("scalar", s_pe, tr_pe_guard)

                def dv(fn):
                    m = inc("vector", s_dve, fn)
                    wt("vector", s_dve, m)
                    return m

                dv(lambda e: e.reduce_max(rowsc[:, 0:1], psb[0][:, 0:8],
                                          axis=AX.X))
                m2 = dv(lambda e: e.tensor_single_scalar(
                    probs[:], psb[0][:, 0:8], rowsc[:, 0:1], ALU.subtract))
                wt("scalar", s_dve, m2)
                pex = inc("scalar", s_act, lambda e: e.activation(
                    probs[:], probs[:], AF.Exp))
                m3 = inc("scalar", s_act, lambda e: e.activation(
                    sgl[:], psb[1][:, 0:1], AF.Exp, scale=-1.0))
                wt("vector", s_act, m3)
                dv(lambda e: e.reduce_sum(rowsc[:, 0:1], probs[:], axis=AX.X))
                dv(lambda e: e.tensor_scalar(sgl[:], sgl[:], 1.0, None,
                                             ALU.add))
                dv(lambda e: e.reciprocal(sgl[:], sgl[:]))
                dv(lambda e: e.reciprocal(rowsc[:, 1:2], rowsc[:, 0:1]))
                dv(lambda e: e.tensor_single_scalar(
                    probs[:], probs[:], rowsc[:, 1:2], ALU.mult))
                dv(lambda e: e.reduce_max(rowsc[:, 0:1], probs[:], axis=AX.X))
                dv(lambda e: e.tensor_single_scalar(
                    pr_sc[:], probs[:], rowsc[:, 0:1], ALU.is_lt))
                dv(lambda e: e.tensor_mul(pr_sc[:], pr_sc[:], probs[:]))
                dv(lambda e: e.reduce_max(rowsc[:, 1:2], pr_sc[:], axis=AX.X))
                dv(lambda e: e.tensor_single_scalar(
                    pr_sc[:], probs[:], rowsc[:, 1:2], ALU.is_ge))
                rt_ps_guard = dv(lambda e: e.tensor_mul(
                    pr_sc[:], pr_sc[:], probs[:]))
                wt("tensor", s_dve, rt_ps_guard)
                wt("tensor", s_act, tr_ps_guard)
                for ee in range(8):
                    ev("tensor", lambda e, ee=ee, pb=psb[2 + ee // 4]:
                       e.transpose(pb[0:1, (ee % 4) * 128:(ee % 4) * 128 + 128],
                                   pr_sc[:, ee:ee + 1], ident[:]))
                tr_pe_guard = inc("tensor", s_pe, lambda e: e.transpose(
                    psb[4][0:1, 0:128], sgl[:, 0:1], ident[:]))
                wt("scalar", s_pe, tr_pe_guard)
                wt("scalar", s_row, row_stage_guard)
                for ee in range(8):
                    ev("scalar", lambda e, ee=ee, pb=psb[2 + ee // 4]: e.copy(
                        stage_row[:, ee * 128:(ee + 1) * 128],
                        pb[0:1, (ee % 4) * 128:(ee % 4) * 128 + 128]))
                tr_ps_guard = inc("scalar", s_act, lambda e: e.copy(
                    stage_row[:, 8 * 128:9 * 128], psb[4][0:1, 0:128]))
                wt("sync", s_act, tr_ps_guard)
                for ee in range(9):
                    row_stage_guard = inc(
                        "sync", s_row, lambda e, ee=ee, grp=grp: e.dma_start(
                            out=wrow_dram[ee, grp * 128:(grp + 1) * 128]
                            .rearrange("(o x) -> o x", o=1),
                            in_=stage_row[:, ee * 128:(ee + 1) * 128]), 16)
            router_done = row_stage_guard

            for ei in range(NEXP):
                wr_i = min(ei, 8)
                wt("sync", s_row, router_done)
                wt("sync", s_pe, wrow_guard)
                rw = inc("sync", s_row, lambda e, wr_i=wr_i: e.dma_start(
                    out=wrow1[:], in_=wrow_dram[wr_i].rearrange("(o x) -> o x", o=1)),
                    16)
                wt("tensor", s_row, rw)
                wt("tensor", s_dve, accg[0][0])
                wt("tensor", s_dve, accg[0][1])
                wt("tensor", s_act, ln_state["bc_guard"])
                ev("tensor", lambda e: e.matmul(
                    psb[4][:], ones_row[:], wrow1[:, 0:512], start=True,
                    stop=True))
                btm = inc("tensor", s_pe, lambda e: e.matmul(
                    psb[5][:], ones_row[:], wrow1[:, 512:1024], start=True,
                    stop=True))
                wrow_guard = btm
                wt("vector", s_pe, btm)
                ev("vector", lambda e: e.tensor_copy(wtil[:, 0:512], psb[4][:]))
                wv_m = inc("vector", s_dve, lambda e: e.tensor_copy(
                    wtil[:, 512:1024], psb[5][:]))
                accg[0] = [wv_m, wv_m]
                def stream_w13(jb, ei=ei):
                    par = w13_pc[0] % 2
                    w13_pc[0] += 1
                    wt("gpsimd", s_pe, w13_free[par])
                    inc("gpsimd", s_w[par], lambda e, ei=ei, jb=jb, par=par:
                        e.dma_start(out=wq[par][:, 0:DC * 128],
                                    in_=w13T[ei, jb]), 16)
                    wm = inc("gpsimd", s_w[par],
                             lambda e, ei=ei, jb=jb, par=par: e.dma_start(
                                 out=wq[par][:, DC * 128:DC * 256],
                                 in_=w13T[ei, JC + jb]), 16)
                    w13_ready[jb] = (par, wm)

                def stream_w2(mb, ei=ei):
                    par = w2_pc[0] % 3
                    w2_pc[0] += 1
                    wt("gpsimd", s_pe, w2_free[par])
                    wm = inc("gpsimd", s_w2[par],
                             lambda e, ei=ei, mb=mb, par=par: e.dma_start(
                                 out=w2q[par][:], in_=w2T[ei, mb]),
                             16)
                    w2_ready[mb] = (par, wm)

                stream_w13(0)
                stream_w13(1)
                stream_w2(0)
                stream_w2(1)
                stream_w2(2)
                for jb in range(JC):
                    par, wm = w13_ready[jb]
                    wt("tensor", s_w[par], wm)
                    wt("tensor", s_act, ps_guard_g[0])
                    wt("tensor", s_act, ps_guard_g[1])
                    for kc in range(DC):
                        for tg in range(2):
                            mm = lambda e, tg=tg, kc=kc, par=par: \
                                e.matmul(
                                    psb[tg][:],
                                    wq[par][:, kc * 128:(kc + 1) * 128],
                                    cs(ytg[tg], kc), start=(kc == 0),
                                    stop=(kc == DC - 1))
                            if kc == DC - 1 and tg == 1:
                                gm_ = inc("tensor", s_pe, mm)
                            else:
                                ev("tensor", mm)
                    wt("tensor", s_dve, ps_guard_u[0])
                    wt("tensor", s_dve, ps_guard_u[1])
                    for kc in range(DC):
                        for tg in range(2):
                            mm = lambda e, tg=tg, kc=kc, par=par: \
                                e.matmul(
                                    psb[2 + tg][:],
                                    wq[par][:, DC * 128 + kc * 128:
                                            DC * 128 + (kc + 1) * 128],
                                    cs(ytg[tg], kc), start=(kc == 0),
                                    stop=(kc == DC - 1))
                            if kc == DC - 1 and tg == 1:
                                um_ = inc("tensor", s_pe, mm)
                            else:
                                ev("tensor", mm)
                    w13_free[par] = um_
                    if jb + 2 < JC:
                        stream_w13(jb + 2)
                    wt("scalar", s_pe, gm_)
                    for tg in range(2):
                        wt("scalar", s_dve, silu_guard[tg])
                        sm_ = inc("scalar", s_act,
                                  lambda e, tg=tg: e.activation(
                                      silu_bf[tg][:], psb[tg][:], AF.Silu))
                        ps_guard_g[tg] = sm_
                    wt("vector", s_pe, um_)
                    for tg in range(2):
                        wt("vector", s_act, ps_guard_g[tg])
                        ev("vector", lambda e, tg=tg: e.tensor_mul(
                            tmp_bf[tg][:], psb[2 + tg][:], silu_bf[tg][:]))
                        dm_ = inc("vector", s_dve,
                                  lambda e, jb=jb, tg=tg: e.tensor_mul(
                                      cs(mid[tg], jb), tmp_bf[tg][:],
                                      wtil[:, tg * 512:(tg + 1) * 512]))
                        ps_guard_u[tg] = dm_
                        silu_guard[tg] = dm_
                mid_mark = dm_
                wt("tensor", s_dve, mid_mark)
                accg[1] = [max(accg[1][0], mid_mark),
                           max(accg[1][1], mid_mark)]
                for mb in range(DC):
                    pair = mb % 2
                    bank = 4 if pair == 0 else 2
                    par, wm = w2_ready[mb]
                    wt("tensor", s_w2[par], wm)
                    wt("tensor", s_dve, accg[pair][0])
                    wt("tensor", s_dve, accg[pair][1])
                    for jc in range(JC):
                        for tg in range(2):
                            mm = lambda e, tg=tg, jc=jc, par=par, bank=bank: \
                                e.matmul(
                                    psb[bank + tg][:],
                                    w2q[par][:, jc * 128:(jc + 1) * 128],
                                    mid[tg][:, jc * TILE:(jc + 1) * TILE],
                                    start=(jc == 0), stop=(jc == JC - 1))
                            if jc == JC - 1 and tg == 1:
                                om_ = inc("tensor", s_pe, mm)
                            else:
                                ev("tensor", mm)
                    w2_free[par] = om_
                    if mb + 3 < DC:
                        stream_w2(mb + 3)
                    wt("vector", s_pe, om_)
                    for tg in range(2):
                        wt("vector", s_dve, acc_zero[tg])
                        am_ = inc("vector", s_dve,
                                  lambda e, mb=mb, tg=tg, bank=bank:
                                  e.tensor_add(
                                      cs(acc[tg], mb), cs(acc[tg], mb),
                                      psb[bank + tg][:]))
                        accg[pair][tg] = am_
                ps_guard_u = [max(ps_guard_u[0], accg[1][0]),
                              max(ps_guard_u[1], accg[1][1])]
                acc_mark = am_
            if debug and h == 0:
                wt("sync", s_dve, acc_mark)
                inc("sync", s_out, lambda e: e.dma_start(
                    out=dbg_acc[0], in_=acc[0][:]), 16)
                inc("sync", s_out, lambda e: e.dma_start(
                    out=dbg_acc[1], in_=acc[1][:]), 16)
            for tg in range(2):
                g = 2 * h + tg
                rdy = max(accg[1][tg], accg[0][0], accg[0][1],
                          ld_guard[0], ld_guard[1], y_ready[0], y_ready[1])
                bc4 = emit_ln(acc[tg], (s_dve, rdy), (s_dve, rdy))
                wt("vector", s_act, bc4)
                for c in range(DC):
                    par = c % 2
                    wt("sync", s_dve, ld_guard[par])
                    xm = inc("sync", s_ld[par], lambda e, g=g, c=c, par=par:
                             e.dma_start(out=a_pp[par][:], in_=x1_sp[g, c]),
                             16)
                    ev("vector", lambda e, c=c, tg=tg: e.tensor_mul(
                        t_sb[:], cs(acc[tg], c), bcr_sb[:]))
                    ev("vector", lambda e, c=c: e.tensor_sub(
                        t_sb[:], t_sb[:], bcmr_sb[:]))
                    wt("vector", s_ld[par], xm)
                    wt("vector", s_out, out_guard[par])
                    om2 = inc("vector", s_dve, lambda e, c=c, par=par:
                              e.tensor_add(b_pp[par][:], t_sb[:],
                                           a_pp[par][:]))
                    ld_guard[par] = om2
                    acc_free_marks[tg][c] = om2
                    wt("sync", s_dve, om2)
                    for h0, h1 in ((0, 64), (64, 128)):
                        inc("sync", s_out,
                            lambda e, g=g, c=c, par=par, h0=h0, h1=h1:
                            e.dma_start(out=out_dram[g, c, h0:h1],
                                        in_=b_pp[par][h0:h1]), 16)
                    out_guard[par] = s_out.n
                ln_state["bcr_free"] = om2

        if debug:
            wt("sync", s_dve, comb_done)
            inc("sync", s_out, lambda e: e.dma_start(
                out=dbg[:, 0:48], in_=summ[:]), 16)
            inc("sync", s_out, lambda e: e.dma_start(
                out=dbg[:, 48:64], in_=a_in[:]), 16)
            inc("sync", s_out, lambda e: e.dma_start(
                out=dbg[:, 64:80], in_=b_in[:]), 16)
        wt("sync", s_out, s_out.n)
        P.run(nc)

    return nc


def _stage_inputs(inputs):
    x = np.asarray(inputs["L0_attn_x"], dtype=np.float32)

    def T(a):
        return np.ascontiguousarray(np.asarray(a, dtype=np.float32).T)

    def vec16(a):
        return np.ascontiguousarray(
            np.asarray(a, dtype=np.float32).reshape(DC, 128).T)

    def wstage(wT):
        return np.ascontiguousarray(
            wT.reshape(DC, 128, 8, 256).transpose(2, 1, 0, 3)
            .reshape(8, 128, DC * 256))

    common = {
        "wkT": wstage(T(inputs["L0_attn_Wk"])),
        "wvT": wstage(T(inputs["L0_attn_Wv"])),
        "wrT": wstage(T(inputs["L0_attn_Wr"])),
        "woT": wstage(T(inputs["L0_attn_Wo"])),
        "bk": vec16(inputs["L0_attn_bk"]), "bv": vec16(inputs["L0_attn_bv"]),
        "br": vec16(inputs["L0_attn_br"]), "bo": vec16(inputs["L0_attn_bo"]),
        "tmk": vec16(inputs["L0_attn_tmk"]), "tmv": vec16(inputs["L0_attn_tmv"]),
        "tmr": vec16(inputs["L0_attn_tmr"]), "uu": vec16(inputs["L0_attn_u"]),
        "ww": vec16(inputs["L0_attn_w"]),
        "ident": np.eye(128, dtype=np.float32),
        "gateT": np.ascontiguousarray(
            T(inputs["L0_ffn_gate_w"]).reshape(DC, 128, 8)),
        "sgwT": vec16(inputs["L0_ffn_shared_gate_w"]),
    }
    w13 = np.asarray(inputs["L0_ffn_w13"], dtype=np.float32)
    w2 = np.asarray(inputs["L0_ffn_w2"], dtype=np.float32)
    sguT = np.asarray(inputs["L0_ffn_shared_gate_up_w"], dtype=np.float32).T
    sdwT = np.asarray(inputs["L0_ffn_shared_down_w"], dtype=np.float32).T
    w13T = np.empty((NEXP, D, 2 * JDIM), np.float32)
    w2T = np.empty((NEXP, JDIM, D), np.float32)
    for e in range(8):
        w13T[e] = w13[e].T
        w2T[e] = w2[e].T
    for q in range(4):
        w13T[8 + q, :, 0:JDIM] = sguT[:, q * JDIM:(q + 1) * JDIM]
        w13T[8 + q, :, JDIM:] = sguT[:, 5632 + q * JDIM:5632 + (q + 1) * JDIM]
        w2T[8 + q] = sdwT[q * JDIM:(q + 1) * JDIM, :]
    common["w13T"] = np.ascontiguousarray(
        w13T.reshape(NEXP, DC, 128, 2 * JC, 128).transpose(0, 3, 2, 1, 4)
        .reshape(NEXP, 2 * JC, 128, DC * 128))
    common["w2T"] = np.ascontiguousarray(
        w2T.reshape(NEXP, JC, 128, DC, 128).transpose(0, 3, 2, 1, 4)
        .reshape(NEXP, DC, 128, JC * 128))

    per_core = []
    for core in range(8):
        b, ch = core // 4, core % 4
        s = ch * TPC
        xTc = x[b, s:s + TPC].T
        xS = np.ascontiguousarray(
            xTc.reshape(DC, 128, NT, TILE).transpose(2, 1, 0, 3)
            .reshape(NT, 128, DC * TILE))
        if s > 0:
            xl = np.ascontiguousarray(x[b, s - 1].reshape(DC, 128).T)
        else:
            xl = np.zeros((128, DC), np.float32)
        ohv = np.zeros((128, 8), np.float32)
        ohv[:, core] = 1.0
        m = {"xT": xS, "xlead_in": xl, "oh": ohv}
        m.update(common)
        per_core.append(m)
    return per_core


def kernel(**inputs):
    per_core = _stage_inputs(inputs)
    nc = build_program()
    res = run_bass_kernel_spmd(nc, per_core, core_ids=list(range(8)))
    out = np.empty((2, 8192, D), np.float32)
    for core in range(8):
        b, ch = core // 4, core % 4
        s = ch * TPC
        oc = res.results[core]["out"]
        out[b, s:s + TPC] = oc.transpose(0, 3, 1, 2).reshape(TPC, D)
    return out


# version 24 (final)
# speedup vs baseline: 1.0134x; 1.0134x over previous
import numpy as np
from contextlib import ExitStack

import concourse.bass as bass
import concourse.mybir as mybir
from concourse.bass_utils import run_bass_kernel_spmd

F32 = mybir.dt.float32
BF16 = mybir.dt.bfloat16
AF = mybir.ActivationFunctionType
ALU = mybir.AluOpType
AX = mybir.AxisListType

D = 2048
DC = 16
TPC = 2048
TILE = 512
NT = TPC // TILE
NEXP = 12
JDIM = 1408
JC = JDIM // 128
EPS = 1e-5

ENGINES = ("sync", "tensor", "vector", "scalar", "gpsimd")


class Prog:

    def __init__(self):
        self.ops = {e: [] for e in ENGINES}

    def run(self, nc):
        with nc.Block() as blk:
            @blk.sync
            def _(e):
                for f in self.ops["sync"]:
                    f(e)

            @blk.tensor
            def _(e):
                for f in self.ops["tensor"]:
                    f(e)

            @blk.vector
            def _(e):
                for f in self.ops["vector"]:
                    f(e)

            @blk.scalar
            def _(e):
                for f in self.ops["scalar"]:
                    f(e)

            @blk.gpsimd
            def _(e):
                for f in self.ops["gpsimd"]:
                    f(e)


def build_program(debug=False):
    nc = bass.Bass("TRN2", target_bir_lowering=False)

    xT = nc.dram_tensor("xT", [NT, 128, DC * TILE], F32,
                        kind="ExternalInput").ap()
    xlead_in = nc.dram_tensor("xlead_in", [128, DC], F32, kind="ExternalInput").ap()
    wT_in = {w: nc.dram_tensor(f"w{w}T", [8, 128, DC * 256], F32,
                               kind="ExternalInput").ap()
             for w in "kvro"}
    vec_names = ["bk", "bv", "br", "bo", "tmk", "tmv", "tmr", "uu", "ww"]
    vecs_in = {n: nc.dram_tensor(n, [128, DC], F32, kind="ExternalInput").ap()
               for n in vec_names}
    oh_in = nc.dram_tensor("oh", [128, 8], F32, kind="ExternalInput").ap()
    ident_in = nc.dram_tensor("ident", [128, 128], F32, kind="ExternalInput").ap()
    gateT_in = nc.dram_tensor("gateT", [DC, 128, 8], F32, kind="ExternalInput").ap()
    sgwT_in = nc.dram_tensor("sgwT", [128, DC], F32, kind="ExternalInput").ap()
    w13T = nc.dram_tensor("w13T", [NEXP, 2 * JC, 128, DC * 128], F32,
                          kind="ExternalInput").ap()
    w2T = nc.dram_tensor("w2T", [NEXP, DC, 128, JC * 128], F32,
                         kind="ExternalInput").ap()
    out_dram = nc.dram_tensor("out", [NT, DC, 128, TILE], F32,
                              kind="ExternalOutput").ap()

    sk = "ExternalOutput" if debug else "Internal"
    a_sp = nc.dram_tensor("a_sp", [NT, DC, 128, TILE], F32, kind=sk).ap()
    b_sp = nc.dram_tensor("b_sp", [NT, DC, 128, TILE], F32, kind=sk).ap()
    p_sp = nc.dram_tensor("p_sp", [NT, DC, 128, TILE], BF16, kind=sk).ap()
    r_sp = nc.dram_tensor("r_sp", [NT, DC, 128, TILE], BF16, kind=sk).ap()
    x1_sp = nc.dram_tensor("x1_sp", [NT, DC, 128, TILE], F32, kind=sk).ap()
    wrow_dram = nc.dram_tensor("wrow_dram", [9, 1024], F32, kind=sk).ap()
    cc_in = nc.dram_tensor("cc_in", [128, 48], F32)
    cc_out = nc.dram_tensor("cc_out", [8 * 128, 48], F32)
    if debug:
        dbg = nc.dram_tensor("dbg", [128, 128], F32, kind="ExternalOutput").ap()
        dbg_xn = nc.dram_tensor("dbg_xn", [128, DC * (TILE + 1)], BF16,
                                kind="ExternalOutput").ap()
        dbg_uk = nc.dram_tensor("dbg_uk", [128, DC * TILE], F32,
                                kind="ExternalOutput").ap()
        dbg_mix = nc.dram_tensor("dbg_mix", [128, DC * TILE], BF16,
                                 kind="ExternalOutput").ap()
        dbg_xc = nc.dram_tensor("dbg_xc", [128, DC], BF16,
                                kind="ExternalOutput").ap()
        dbg_lr = nc.dram_tensor("dbg_lr", [1, 8], F32,
                                kind="ExternalOutput").ap()
        dbg_acc = nc.dram_tensor("dbg_acc", [2, 128, DC * TILE], F32,
                                 kind="ExternalOutput").ap()

    es = ExitStack()

    def sbt(name, p, f, dt):
        return es.enter_context(nc.sbuf_tensor(name, [p, f], dt))

    with es:
        vt = {n: sbt("c_" + n, 128, DC, F32) for n in vec_names}
        ubk = sbt("c_ubk", 128, DC, F32)
        oh = sbt("c_oh", 128, 8, F32)
        ident = sbt("c_ident", 128, 128, F32)
        gate_bf = sbt("c_gate", 128, DC * 8, BF16)
        sgw_bf = sbt("c_sgw", 128, DC, BF16)
        ones_col = sbt("c_onc", 128, 1, F32)
        ones_row = sbt("c_onr", 1, 128, F32)
        xlead = sbt("xlead", 128, DC, F32)
        xn_carry = sbt("xn_carry", 128, DC, BF16)
        carry_a = sbt("carry_a", 128, DC, F32)
        carry_b = sbt("carry_b", 128, DC, F32)
        carry_c = sbt("carry_c", 128, DC, F32)
        a_in = sbt("a_in", 128, DC, F32)
        b_in = sbt("b_in", 128, DC, F32)
        gath = sbt("gath", 128, 8 * 48, F32)
        summ = sbt("summ", 128, 48, F32)
        st8 = sbt("st8", 128, 4 * DC, F32)

        AR1 = sbt("AR1", 128, DC * TILE, F32)
        AR2 = sbt("AR2", 128, DC * TILE, F32)
        AR3 = sbt("AR3", 128, DC * TILE, BF16)
        AR4 = sbt("AR4", 128, DC * TILE, BF16)
        AR5 = sbt("AR5", 128, DC * (TILE + 1), BF16)
        AR6 = sbt("AR6", 128, JC * TILE, BF16)
        wq = [sbt(f"wq{i}", 128, DC * 256, BF16) for i in range(2)]
        w2q = [sbt(f"w2q{i}", 128, JC * 128, BF16) for i in range(3)]
        t_sb = sbt("t_sb", 128, TILE, F32)
        e1a_sb = sbt("e1a_sb", 128, TILE, F32)
        e1_sb = sbt("e1_sb", 128, TILE, F32)
        e2a_sb = sbt("e2a_sb", 128, TILE, F32)
        e2_sb = sbt("e2_sb", 128, TILE, F32)
        e2v_sb = sbt("e2v_sb", 128, TILE, F32)
        a_pp = [sbt(f"a_pp{i}", 128, TILE, F32) for i in range(2)]
        b_pp = [sbt(f"b_pp{i}", 128, TILE, F32) for i in range(2)]
        cum_pp = [sbt(f"cum_pp{i}", 128, TILE, F32) for i in range(2)]
        p_pp = [sbt(f"p_pp{i}", 128, TILE, BF16) for i in range(2)]
        r_pp = [sbt(f"r_pp{i}", 128, TILE, BF16) for i in range(2)]
        rows = sbt("rows", 1, 3 * TILE, F32)
        d2_bf = sbt("d2bf", 128, TILE, BF16)
        bcr_sb = sbt("bcr", 128, TILE, F32)
        bcmr_sb = sbt("bcmr", 128, TILE, F32)
        lead_rows = sbt("lead_rows", 1, 8, F32)
        eps_row = sbt("eps_row", 1, 1, F32)
        lead_bc = sbt("lead_bc", 128, 2, F32)
        probs = sbt("probs", 128, 8, F32)
        pr_sc = sbt("pr_sc", 128, 8, F32)
        rowsc = sbt("rowsc", 128, 2, F32)
        sgl = sbt("sgl", 128, 1, F32)
        stage_row = sbt("stage_row", 1, 9 * 128, F32)
        wrow1 = sbt("wrow1", 1, 1024, F32)
        wtil = cum_pp[0][:].bitcast(BF16)

        sqp = [t_sb, e1a_sb]

        class Sem:
            def __init__(self, name):
                self.h = es.enter_context(nc.semaphore(name))
                self.n = 0

        s_cn = Sem("s_cn")
        s_x = Sem("s_x")
        s_w = [Sem("s_w0"), Sem("s_w1")]
        s_w2 = [Sem("s_w20"), Sem("s_w21"), Sem("s_w22")]
        s_sp = [Sem("s_sp0"), Sem("s_sp1")]
        s_ld = [Sem("s_ld0"), Sem("s_ld1")]
        s_pe = Sem("s_pe")
        s_dve = Sem("s_dve")
        s_act = Sem("s_act")
        s_cc = Sem("s_cc")
        s_row = Sem("s_row")
        s_out = Sem("s_out")

        P = Prog()

        def ev(eng, fn):
            P.ops[eng].append(fn)

        def wt(eng, sem, n):
            if n > 0:
                P.ops[eng].append(lambda e, s=sem.h, v=n: e.wait_ge(s, v))

        def inc(eng, sem, fn, k=1):
            sem.n += k
            P.ops[eng].append(lambda e, f=fn, s=sem.h, kk=k: f(e).then_inc(s, kk))
            return sem.n


        def cs(buf, c, w=TILE):
            return buf[:, c * w:(c + 1) * w]

        def xn3():
            return AR5[:].rearrange("p (c j) -> p c j", c=DC)

        for n in vec_names:
            inc("sync", s_cn, lambda e, d=vt[n], s=vecs_in[n]:
                e.dma_start(out=d[:], in_=s), 16)
        inc("sync", s_cn, lambda e: e.dma_start(out=oh[:], in_=oh_in), 16)
        inc("sync", s_cn, lambda e: e.dma_start(out=ident[:], in_=ident_in), 16)
        inc("sync", s_cn, lambda e: e.dma_start(
            out=xlead[:], in_=xlead_in), 16)
        inc("gpsimd", s_cn, lambda e: e.dma_start(
            out=gate_bf[:].rearrange("p (c e) -> p c e", c=DC),
            in_=gateT_in.rearrange("c p e -> p c e")), 16)
        inc("gpsimd", s_cn, lambda e: e.dma_start(
            out=sgw_bf[:], in_=sgwT_in), 16)
        cn_done = s_cn.n
        wt("vector", s_cn, cn_done)
        ev("vector", lambda e: e.tensor_add(ubk[:], vt["uu"][:], vt["bk"][:]))
        ev("vector", lambda e: e.memset(ones_col[:], 1.0))
        ev("vector", lambda e: e.memset(ones_row[:], 1.0))
        ev("vector", lambda e: e.memset(eps_row[:], EPS))
        ev("vector", lambda e: e.memset(carry_a[:], 0.0))
        ev("vector", lambda e: e.memset(carry_b[:], 0.0))
        m_init = inc("vector", s_dve, lambda e: e.memset(carry_c[:], 0.0))
        wt("scalar", s_cn, cn_done)
        m_sq = inc("scalar", s_act, lambda e: e.activation(
            sqp[0][:, 0:DC], xlead[:], AF.Square))
        wt("tensor", s_dve, m_init)
        wt("tensor", s_act, m_sq)
        ps_s1 = es.enter_context(nc.psum_tensor("ps_s1", [1, TILE], F32))
        ps_s2 = es.enter_context(nc.psum_tensor("ps_s2", [1, TILE], F32))
        psb = [es.enter_context(nc.psum_tensor(f"psb{i}", [128, 512], F32))
               for i in range(6)]
        m_lead = inc("tensor", s_pe, lambda e: e.matmul(
            ps_s1[:, 0:DC], ones_col[:], xlead[:], start=True, stop=True))
        m_lead = inc("tensor", s_pe, lambda e: e.matmul(
            ps_s2[:, 0:DC], ones_col[:], sqp[0][:, 0:DC], start=True, stop=True))
        wt("vector", s_pe, m_lead)
        hs = inc("vector", s_dve, lambda e: e.reduce_sum(
            lead_rows[:, 0:1], ps_s1[:, 0:DC], axis=AX.X))
        hs = inc("vector", s_dve, lambda e: e.reduce_sum(
            lead_rows[:, 1:2], ps_s2[:, 0:DC], axis=AX.X))
        wt("vector", s_dve, hs)
        ev("vector", lambda e: e.tensor_scalar(
            lead_rows[:, 2:3], lead_rows[:, 0:1], 1.0 / D, None, ALU.mult))
        hs = inc("vector", s_dve, lambda e: e.tensor_scalar(
            lead_rows[:, 3:4], lead_rows[:, 1:2], 1.0 / D, None, ALU.mult))
        wt("vector", s_dve, hs)
        hs = inc("vector", s_dve, lambda e: e.tensor_mul(
            lead_rows[:, 4:5], lead_rows[:, 2:3], lead_rows[:, 2:3]))
        wt("vector", s_dve, hs)
        hs = inc("vector", s_dve, lambda e: e.tensor_sub(
            lead_rows[:, 4:5], lead_rows[:, 3:4], lead_rows[:, 4:5]))
        wt("vector", s_dve, hs)
        m_var = inc("vector", s_dve, lambda e: e.tensor_scalar(
            lead_rows[:, 4:5], lead_rows[:, 4:5], 0.0, None, ALU.max))
        wt("scalar", s_dve, m_var)
        hsa = inc("scalar", s_act, lambda e: e.activation(
            lead_rows[:, 5:6], lead_rows[:, 4:5], AF.Ln, bias=eps_row[0:1, 0:1]))
        wt("scalar", s_act, hsa)
        m_rst = inc("scalar", s_act, lambda e: e.activation(
            lead_rows[:, 5:6], lead_rows[:, 5:6], AF.Exp, scale=-0.5))
        wt("vector", s_act, m_rst)
        m_mr = inc("vector", s_dve, lambda e: e.tensor_mul(
            lead_rows[:, 6:7], lead_rows[:, 2:3], lead_rows[:, 5:6]))
        wt("tensor", s_dve, m_mr)
        m_bc = inc("tensor", s_pe, lambda e: e.matmul(
            psb[0][:, 0:2], ones_row[:], lead_rows[:, 5:7], start=True, stop=True))
        wt("vector", s_pe, m_bc)
        m_cp = inc("vector", s_dve, lambda e: e.tensor_copy(lead_bc[:],
                                                            psb[0][:, 0:2]))
        wt("vector", s_dve, m_cp)
        m_xc = inc("vector", s_dve, lambda e: e.tensor_scalar(
            xn_carry[:], xlead[:], lead_bc[:, 0:1], lead_bc[:, 1:2],
            ALU.mult, ALU.subtract))
        if debug:
            wt("sync", s_dve, m_xc)
            inc("sync", s_out, lambda e: e.dma_start(
                out=dbg_xc, in_=xn_carry[:]), 16)
            inc("sync", s_out, lambda e: e.dma_start(
                out=dbg_lr, in_=lead_rows[:]), 16)
        psb0_free = m_cp

        ln_state = {"sq_guard": [0, 0], "bc_guard": 0, "row_guard": 0,
                    "stat_guard": 0, "bcr_free": 0}

        def ln_stats_chunk(buf, c, sqb=None, sq_eng="scalar",
                           ready_act=(), ready_pe=()):
            sq = sqp[c % 2] if sqb is None else sqb[c % 2]
            wt(sq_eng, s_pe, ln_state["sq_guard"][c % 2])
            for sm, mk in ready_act:
                wt(sq_eng, sm, mk)
            if sq_eng == "scalar":
                sqm = inc(sq_eng, s_act, lambda e, c=c, sq=sq: e.activation(
                    sq[:], cs(buf, c), AF.Square))
            else:
                sqm = inc(sq_eng, s_act, lambda e, c=c, sq=sq: e.tensor_mul(
                    sq[:], cs(buf, c), cs(buf, c)))
            for sm, mk in ready_pe:
                wt("tensor", sm, mk)
            if c == 0:
                wt("tensor", s_dve, ln_state["row_guard"])
            ev("tensor", lambda e, c=c: e.matmul(
                ps_s1[:], ones_col[:], cs(buf, c),
                start=(c == 0), stop=(c == DC - 1)))
            wt("tensor", s_act, sqm)
            pm = inc("tensor", s_pe, lambda e, c=c, sq=sq: e.matmul(
                ps_s2[:], ones_col[:], sq[:],
                start=(c == 0), stop=(c == DC - 1)))
            ln_state["sq_guard"][c % 2] = pm
            return pm

        def emit_ln(buf, ready_pe, ready_act):
            ra = [ready_act] if isinstance(ready_act, tuple) else ready_act
            rp = [ready_pe] if isinstance(ready_pe, tuple) else ready_pe
            for c in range(DC):
                stat_mark = ln_stats_chunk(buf, c,
                                           ready_act=(ra if c == 0 else ()),
                                           ready_pe=(rp if c == 0 else ()))
            return ln_bc(ln_rows(stat_mark))

        def ln_finish(stat_mark, rows_eng="vector"):
            return ln_bc(ln_rows(stat_mark, rows_eng))

        def ln_rows(stat_mark, eng="vector"):
            wt("vector", s_pe, stat_mark)
            ev("vector", lambda e: e.tensor_scalar(
                rows[:, 0:TILE], ps_s1[:], 1.0 / D, None, ALU.mult))
            pr = inc("vector", s_dve, lambda e: e.tensor_scalar(
                rows[:, TILE:2 * TILE], ps_s2[:], 1.0 / D, None, ALU.mult))
            if eng != "vector":
                wt(eng, s_dve, pr)
            ev(eng, lambda e: e.tensor_mul(
                rows[:, 2 * TILE:3 * TILE], rows[:, 0:TILE], rows[:, 0:TILE]))
            ev(eng, lambda e: e.tensor_sub(
                rows[:, TILE:2 * TILE], rows[:, TILE:2 * TILE],
                rows[:, 2 * TILE:3 * TILE]))
            row_mark = inc(eng, s_dve, lambda e: e.tensor_scalar(
                rows[:, TILE:2 * TILE], rows[:, TILE:2 * TILE], 0.0, None,
                ALU.max))
            ln_state["row_guard"] = row_mark
            wt("scalar", s_dve, row_mark)
            ev("scalar", lambda e: e.activation(
                rows[:, TILE:2 * TILE], rows[:, TILE:2 * TILE], AF.Ln, bias=eps_row[0:1, 0:1]))
            rst_mark = inc("scalar", s_act, lambda e: e.activation(
                rows[:, TILE:2 * TILE], rows[:, TILE:2 * TILE], AF.Exp,
                scale=-0.5))
            wt(eng, s_act, rst_mark)
            mr_mark = inc(eng, s_dve, lambda e: e.tensor_mul(
                rows[:, 2 * TILE:3 * TILE], rows[:, 0:TILE],
                rows[:, TILE:2 * TILE]))
            return mr_mark

        def ln_bc(mr_mark):
            wt("tensor", s_dve, mr_mark)
            wt("tensor", s_act, ln_state["bc_guard"])
            ev("tensor", lambda e: e.matmul(
                psb[4][:], ones_row[:], rows[:, TILE:2 * TILE],
                start=True, stop=True))
            bcm = inc("tensor", s_pe, lambda e: e.matmul(
                psb[5][:], ones_row[:], rows[:, 2 * TILE:3 * TILE],
                start=True, stop=True))
            wt("scalar", s_pe, bcm)
            wt("scalar", s_dve, ln_state["bcr_free"])
            ev("scalar", lambda e: e.copy(bcr_sb[:], psb[4][:]))
            bc_mark = inc("scalar", s_act, lambda e: e.copy(bcmr_sb[:], psb[5][:]))
            ln_state["bc_guard"] = bc_mark
            return bc_mark

        wq_ready = {}
        npiece = [0]

        def stream_attn_piece(w, q, gate):
            i = npiece[0]
            par = i % 2
            npiece[0] += 1
            wt("gpsimd", s_pe, gate)
            m = inc("gpsimd", s_w[par], lambda e, w=w, q=q, par=par:
                    e.dma_start(out=wq[par][:], in_=wT_in[w][q]), 16)
            wq_ready[i] = (par, m)
            return i

        xbuf_guard = 0
        uk_marks = [0] * DC
        e2v_marks = [0] * DC
        uk_guard = [0] * DC
        spill_guard = [0, 0]
        rspill = [0, 0]
        wq_gate = [0, 0]
        carry_mark = [m_xc]
        carry_mark_g = [m_init]
        d_bf = e2a_sb[:].bitcast(BF16)[:, 0:TILE]
        prev_wr_pm = 0
        silu_last = 0
        mixk_mark = [0] * NT
        mixv_mark = [0] * NT
        x_marks = [0] * NT

        def emit_xn_mixes(g, bc_mark):
            nonlocal_state = None
            wt("vector", s_act, bc_mark)
            ev("vector", lambda e: e.tensor_copy(
                xn3()[:, :, 0:1], xn_carry[:].rearrange("p (c o) -> p c o", o=1)))
            for c in range(DC):
                ev("vector", lambda e, c=c: e.tensor_mul(
                    t_sb[:], cs(AR1, c), bcr_sb[:]))
                ev("vector", lambda e, c=c: e.tensor_sub(
                    xn3()[:, c, 1:TILE + 1], t_sb[:], bcmr_sb[:]))
            xg = inc("vector", s_dve, lambda e: e.tensor_copy(
                xn_carry[:].rearrange("p (c o) -> p c o", o=1),
                xn3()[:, :, TILE:TILE + 1]))
            ln_state["bcr_free"] = xg
            if g + 1 < NT:
                wt("sync", s_dve, xg)
                x_marks[g + 1] = inc("sync", s_x, lambda e, g=g: e.dma_start(
                    out=AR1[:], in_=xT[g + 1]), 16)
            wt("vector", s_act, silu_last)
            for c in range(DC):
                ev("vector", lambda e, c=c, d=d_bf: e.tensor_sub(
                    d, xn3()[:, c, 1:TILE + 1], xn3()[:, c, 0:TILE]))
                fn = lambda e, c=c, d=d_bf: e.scalar_tensor_tensor(
                    cs(AR3, c), d, vt["tmk"][:, c:c + 1],
                    xn3()[:, c, 0:TILE], ALU.mult, ALU.add)
                if c == DC - 1:
                    mixk_mark[g] = inc("vector", s_dve, fn)
                else:
                    ev("vector", fn)
            wt("vector", s_pe, prev_wr_pm)
            for c in range(DC):
                ev("vector", lambda e, c=c, d=d_bf: e.tensor_sub(
                    d, xn3()[:, c, 1:TILE + 1], xn3()[:, c, 0:TILE]))
                fn = lambda e, c=c, d=d_bf: e.scalar_tensor_tensor(
                    cs(AR4, c), d, vt["tmv"][:, c:c + 1],
                    xn3()[:, c, 0:TILE], ALU.mult, ALU.add)
                if c == DC - 1:
                    mixv_mark[g] = inc("vector", s_dve, fn)
                else:
                    ev("vector", fn)
            return xg

        x_marks[0] = inc("sync", s_x, lambda e: e.dma_start(
            out=AR1[:], in_=xT[0]), 16)
        bc0 = emit_ln(AR1, (s_x, x_marks[0]), (s_x, x_marks[0]))
        xbuf_guard = emit_xn_mixes(0, bc0)

        for g in range(NT):
            have_next = g + 1 < NT
            pieces = [stream_attn_piece("k", 0, wq_gate[0]),
                      stream_attn_piece("k", 1, wq_gate[1])]
            wt("tensor", s_dve, mixk_mark[g])
            for c in range(DC):
                par, wm = wq_ready[pieces[c // 2]]
                wt("tensor", s_w[par], wm)
                wt("tensor", s_act, uk_marks[(c - 2) % DC])
                for kc in range(DC):
                    mm = lambda e, c=c, kc=kc, par=par: e.matmul(
                        psb[c % 2][:],
                        wq[par][:, kc * 256 + (c % 2) * 128:
                                kc * 256 + (c % 2) * 128 + 128],
                        cs(AR3, kc), start=(kc == 0), stop=(kc == DC - 1))
                    if kc == DC - 1:
                        pm = inc("tensor", s_pe, mm)
                    else:
                        ev("tensor", mm)
                if c % 2 == 1 and c // 2 + 2 < 8:
                    pieces.append(stream_attn_piece("k", c // 2 + 2, pm))
                if c == 13:
                    pm13 = pm
                wt("scalar", s_pe, pm)
                wt("scalar", s_dve, uk_guard[c])
                uk_marks[c] = inc("scalar", s_act, lambda e, c=c: e.add(
                    cs(AR2, c), psb[c % 2][:], ubk[:, c:c + 1]))
            wq_gate = [pm13, pm]
            wk_pm = pm
            pieces = [stream_attn_piece("v", 0, wq_gate[0]),
                      stream_attn_piece("v", 1, wq_gate[1])]
            wt("tensor", s_dve, mixv_mark[g])
            for c in range(DC):
                par, wm = wq_ready[pieces[c // 2]]
                wt("tensor", s_w[par], wm)
                wt("tensor", s_dve, e2v_marks[(c - 2) % DC])
                for kc in range(DC):
                    mm = lambda e, c=c, kc=kc, par=par: e.matmul(
                        psb[2 + c % 2][:],
                        wq[par][:, kc * 256 + (c % 2) * 128:
                                kc * 256 + (c % 2) * 128 + 128],
                        cs(AR4, kc), start=(kc == 0), stop=(kc == DC - 1))
                    if kc == DC - 1:
                        pm = inc("tensor", s_pe, mm)
                    else:
                        ev("tensor", mm)
                if c % 2 == 1 and c // 2 + 2 < 8:
                    pieces.append(stream_attn_piece("v", c // 2 + 2, pm))
                if c == 13:
                    pm13 = pm
                if have_next and 1 <= c <= 8:
                    for sc in (2 * (c - 1), 2 * (c - 1) + 1):
                        stat_next = ln_stats_chunk(
                            AR1, sc, sqb=[bcr_sb, bcmr_sb],
                            ready_pe=([(s_x, x_marks[g + 1])]
                                      if sc == 0 else ()),
                            ready_act=([(s_x, x_marks[g + 1]),
                                        (s_dve, xbuf_guard)]
                                       if sc == 0 else ()))
                if have_next and c == 9:
                    mr_next = ln_rows(stat_next)
                if have_next and c == 12:
                    bc_next = ln_bc(mr_next)
                wt("vector", s_act, uk_marks[c])
                ev("vector", lambda e, c=c: e.tensor_scalar(
                    t_sb[:], cs(AR2, c), vt["ww"][:, c:c + 1], None, ALU.max))
                ev("vector", lambda e, c=c: e.tensor_scalar(
                    e1a_sb[:], t_sb[:], vt["ww"][:, c:c + 1], -1.0,
                    ALU.add, ALU.mult))
                pr_mark = inc("vector", s_dve, lambda e, c=c: e.tensor_sub(
                    e2a_sb[:], cs(AR2, c), t_sb[:]))
                uk_guard[c] = pr_mark
                wt("scalar", s_dve, pr_mark)
                ev("scalar", lambda e: e.activation(e1_sb[:], e1a_sb[:], AF.Exp))
                ex_mark = inc("scalar", s_act, lambda e: e.activation(
                    e2_sb[:], e2a_sb[:], AF.Exp))
                wt("vector", s_pe, pm)
                wt("vector", s_act, ex_mark)
                e2v_marks[c] = inc("vector", s_dve, lambda e, c=c:
                                   e.scalar_tensor_tensor(
                                       e2v_sb[:], psb[2 + c % 2][:],
                                       vt["bv"][:, c:c + 1], e2_sb[:],
                                       ALU.add, ALU.mult))
                par2 = c % 2
                wt("vector", s_sp[par2], spill_guard[par2])
                wt("vector", s_dve, carry_mark[0])
                ev("vector", lambda e, c=c, p2=par2: e.tensor_tensor_scan(
                    a_pp[p2][:], e1_sb[:], e2v_sb[:], carry_a[:, c:c + 1],
                    ALU.mult, ALU.add))
                ev("vector", lambda e, c=c, p2=par2: e.tensor_tensor_scan(
                    b_pp[p2][:], e1_sb[:], e2_sb[:], carry_b[:, c:c + 1],
                    ALU.mult, ALU.add))
                ev("vector", lambda e, c=c, p2=par2: e.tensor_tensor_scan(
                    cum_pp[p2][:], e1a_sb[:], e1a_sb[:], carry_c[:, c:c + 1],
                    ALU.add, ALU.bypass))
                ev("vector", lambda e, c=c, p2=par2: e.tensor_copy(
                    carry_a[:, c:c + 1], a_pp[p2][:, TILE - 1:TILE]))
                ev("vector", lambda e, c=c, p2=par2: e.tensor_copy(
                    carry_b[:, c:c + 1], b_pp[p2][:, TILE - 1:TILE]))
                cm = inc("vector", s_dve, lambda e, c=c, p2=par2: e.tensor_copy(
                    carry_c[:, c:c + 1], cum_pp[p2][:, TILE - 1:TILE]))
                carry_mark[0] = cm
                carry_mark_g[0] = cm
                if c == 0:
                    wt("vector", s_pe, wk_pm)
                ev("vector", lambda e, c=c: e.tensor_sub(
                    d2_bf[:], xn3()[:, c, 1:TILE + 1], xn3()[:, c, 0:TILE]))
                mr_c = inc("vector", s_dve, lambda e, c=c:
                           e.scalar_tensor_tensor(
                               cs(AR3, c), d2_bf[:], vt["tmr"][:, c:c + 1],
                               xn3()[:, c, 0:TILE], ALU.mult, ALU.add))
                wt("scalar", s_dve, cm)
                wt("scalar", s_sp[par2], spill_guard[par2])
                pexp = inc("scalar", s_act, lambda e, p2=par2: e.activation(
                    p_pp[p2][:], cum_pp[p2][:], AF.Exp))
                if c < JC:
                    if c == 0:
                        wt("scalar", s_pe, prev_wr_pm)
                    wt("scalar", s_dve, mr_c)
                    sl = inc("scalar", s_act, lambda e, c=c: e.activation(
                        cs(AR6, c), cs(AR3, c), AF.Silu))
                wt("sync", s_dve, cm)
                wt("sync", s_act, pexp)
                inc("sync", s_sp[par2], lambda e, g=g, c=c, p2=par2:
                    e.dma_start(out=a_sp[g, c], in_=a_pp[p2][:]), 16)
                inc("sync", s_sp[par2], lambda e, g=g, c=c, p2=par2:
                    e.dma_start(out=b_sp[g, c], in_=b_pp[p2][:]), 16)
                sm = inc("sync", s_sp[par2], lambda e, g=g, c=c, p2=par2:
                         e.dma_start(out=p_sp[g, c], in_=p_pp[p2][:]), 16)
                spill_guard[par2] = sm
            wq_gate = [pm13, pm]
            wv_pm = pm
            mixr_last = mr_c
            wt("scalar", s_pe, wv_pm)
            wt("scalar", s_dve, mixr_last)
            for c in range(JC, DC):
                sl = inc("scalar", s_act, lambda e, c=c: e.activation(
                    cs(AR4, c), cs(AR3, c), AF.Silu))
            silu_last = sl
            pieces = [stream_attn_piece("r", 0, wq_gate[0]),
                      stream_attn_piece("r", 1, wq_gate[1])]
            wt("tensor", s_act, silu_last)
            for c in range(DC):
                par, wm = wq_ready[pieces[c // 2]]
                wt("tensor", s_w[par], wm)
                wt("tensor", s_act, uk_marks[(c - 2) % DC])
                for kc in range(DC):
                    mov = (cs(AR6, kc) if kc < JC else cs(AR4, kc))
                    mm = lambda e, c=c, kc=kc, par=par, mov=mov: e.matmul(
                        psb[c % 2][:],
                        wq[par][:, kc * 256 + (c % 2) * 128:
                                kc * 256 + (c % 2) * 128 + 128],
                        mov, start=(kc == 0), stop=(kc == DC - 1))
                    if kc == DC - 1:
                        pm = inc("tensor", s_pe, mm)
                    else:
                        ev("tensor", mm)
                if c % 2 == 1 and c // 2 + 2 < 8:
                    pieces.append(stream_attn_piece("r", c // 2 + 2, pm))
                if c == 13:
                    pm13 = pm
                wt("scalar", s_pe, pm)
                wt("scalar", s_sp[c % 2], rspill[c % 2])
                uk_marks[c] = em = inc("scalar", s_act, lambda e, c=c: e.add(
                    r_pp[c % 2][:], psb[c % 2][:], vt["br"][:, c:c + 1]))
                wt("sync", s_act, em)
                rm = inc("sync", s_sp[c % 2], lambda e, g=g, c=c:
                         e.dma_start(out=r_sp[g, c], in_=r_pp[c % 2][:]), 16)
                rspill[c % 2] = rm
            wq_gate = [pm13, pm]
            prev_wr_pm = pm
            if have_next:
                xbuf_guard = emit_xn_mixes(g + 1, bc_next)
        mix_guard = prev_wr_pm

        P.run(nc)
        P = Prog()

        wt("scalar", s_dve, max(carry_mark[0], carry_mark_g[0]))
        sA = inc("scalar", s_act, lambda e: e.activation(
            summ[:, 0:DC], carry_c[:], AF.Exp))
        ev("vector", lambda e: e.tensor_copy(summ[:, DC:2 * DC], carry_a[:]))
        sB = inc("vector", s_dve, lambda e: e.tensor_copy(
            summ[:, 2 * DC:3 * DC], carry_b[:]))
        wt("sync", s_act, sA)
        wt("sync", s_dve, sB)
        scc = inc("sync", s_x, lambda e: e.dma_start(out=cc_in.ap(), in_=summ[:]),
                  16)
        wt("gpsimd", s_x, scc)
        ccm = inc("gpsimd", s_cc, lambda e: e.collective_compute(
            "AllGather", ALU.bypass, replica_groups=[list(range(8))],
            ins=[cc_in.ap().opt()], outs=[cc_out.ap().opt()]))
        wt("gpsimd", s_cc, ccm)
        gm = inc("gpsimd", s_cc, lambda e: e.dma_start(
            out=gath[:].rearrange("p (r c) -> p r c", r=8),
            in_=cc_out.ap().rearrange("(r p) c -> p r c", p=128)), 16)
        wt("vector", s_cc, gm)
        for r in range(8):
            if r % 4 == 0:
                ev("vector", lambda e: e.memset(st8[:, 0:2 * DC], 0.0))
            if r == 0:
                ev("vector", lambda e: e.memset(st8[:, 2 * DC:4 * DC], 0.0))
            ev("vector", lambda e, r=r: e.scalar_tensor_tensor(
                st8[:, 2 * DC:3 * DC], st8[:, 0:DC], oh[:, r:r + 1],
                st8[:, 2 * DC:3 * DC], ALU.mult, ALU.add))
            ev("vector", lambda e, r=r: e.scalar_tensor_tensor(
                st8[:, 3 * DC:4 * DC], st8[:, DC:2 * DC], oh[:, r:r + 1],
                st8[:, 3 * DC:4 * DC], ALU.mult, ALU.add))
            if r < 7:
                g0 = 48 * r
                ev("vector", lambda e, g0=g0: e.tensor_mul(
                    st8[:, 0:DC], st8[:, 0:DC], gath[:, g0:g0 + DC]))
                ev("vector", lambda e, g0=g0: e.tensor_add(
                    st8[:, 0:DC], st8[:, 0:DC], gath[:, g0 + DC:g0 + 2 * DC]))
                ev("vector", lambda e, g0=g0: e.tensor_mul(
                    st8[:, DC:2 * DC], st8[:, DC:2 * DC], gath[:, g0:g0 + DC]))
                ev("vector", lambda e, g0=g0: e.tensor_add(
                    st8[:, DC:2 * DC], st8[:, DC:2 * DC],
                    gath[:, g0 + 2 * DC:g0 + 3 * DC]))
        ev("vector", lambda e: e.tensor_copy(a_in[:], st8[:, 2 * DC:3 * DC]))
        comb_done = inc("vector", s_dve, lambda e: e.tensor_copy(
            b_in[:], st8[:, 3 * DC:4 * DC]))
        P.run(nc)
        P = Prog()

        wt("sync", s_sp[0], s_sp[0].n)
        wt("sync", s_sp[1], s_sp[1].n)
        ld_guard = [0, 0]
        wrbuf = [AR3, AR4]
        wr_pair_guard = [mix_guard, mix_guard]
        att_evac_guard = [0] * DC
        x1_done = [None] * NT
        wr_marks = [0] * NT
        cumsq = [cum_pp[0], cum_pp[1]]

        def emit_corr(g):
            wb = wrbuf[g % 2]
            if g == 0:
                wt("vector", s_dve, comb_done)
            wt("vector", s_pe, wr_pair_guard[g % 2])
            for c in range(DC):
                par = c % 2
                wt("sync", s_dve, ld_guard[par])
                inc("sync", s_ld[par], lambda e, g=g, c=c, par=par:
                    e.dma_start(out=a_pp[par][:], in_=a_sp[g, c]), 16)
                inc("sync", s_ld[par], lambda e, g=g, c=c, par=par:
                    e.dma_start(out=b_pp[par][:], in_=b_sp[g, c]), 16)
                inc("sync", s_ld[par], lambda e, g=g, c=c, par=par:
                    e.dma_start(out=p_pp[par][:], in_=p_sp[g, c]), 16)
                lm = inc("sync", s_ld[par], lambda e, g=g, c=c, par=par:
                         e.dma_start(out=r_pp[par][:], in_=r_sp[g, c]), 16)
                wt("vector", s_ld[par], lm)
                ev("vector", lambda e, c=c, par=par: e.scalar_tensor_tensor(
                    t_sb[:], p_pp[par][:], a_in[:, c:c + 1], a_pp[par][:],
                    ALU.mult, ALU.add))
                ev("vector", lambda e, c=c, par=par:
                   e.scalar_tensor_tensor(
                       e1a_sb[:], p_pp[par][:], b_in[:, c:c + 1],
                       b_pp[par][:], ALU.mult, ALU.add))
                ev("vector", lambda e: e.reciprocal(e2a_sb[:], e1a_sb[:]))
                ev("vector", lambda e: e.tensor_mul(e2v_sb[:], t_sb[:],
                                                    e2a_sb[:]))
                lg = inc("vector", s_dve, lambda e, c=c, par=par, wb=wb:
                         e.tensor_mul(cs(wb, c), e2v_sb[:], r_pp[par][:]))
                ld_guard[par] = lg
            wr_marks[g] = lg

        emit_corr(0)
        for g in range(NT):
            wt("sync", s_dve, xbuf_guard)
            x_mark = inc("sync", s_x, lambda e, g=g: e.dma_start(
                out=AR1[:], in_=xT[g]), 16)
            wb = wrbuf[g % 2]
            pieces = [stream_attn_piece("o", 0, wq_gate[0]),
                      stream_attn_piece("o", 1, wq_gate[1])]
            wt("tensor", s_dve, wr_marks[g])
            for c in range(DC):
                par, wm = wq_ready[pieces[c // 2]]
                wt("tensor", s_w[par], wm)
                wt("tensor", s_act, uk_marks[(c - 2) % DC])
                for kc in range(DC):
                    mm = lambda e, c=c, kc=kc, par=par, wb=wb: e.matmul(
                        psb[c % 2][:],
                        wq[par][:, kc * 256 + (c % 2) * 128:
                                kc * 256 + (c % 2) * 128 + 128],
                        cs(wb, kc), start=(kc == 0), stop=(kc == DC - 1))
                    if kc == DC - 1:
                        pm = inc("tensor", s_pe, mm)
                    else:
                        ev("tensor", mm)
                if c % 2 == 1 and c // 2 + 2 < 8:
                    pieces.append(stream_attn_piece("o", c // 2 + 2, pm))
                if c == 13:
                    pm13 = pm
                wt("scalar", s_pe, pm)
                wt("scalar", s_sp[c % 2], att_evac_guard[c])
                uk_marks[c] = inc("scalar", s_act, lambda e, c=c: e.add(
                    cs(AR2, c), psb[c % 2][:], vt["bo"][:, c:c + 1]))
                if c >= 2:
                    ln_stats_chunk(AR2, c - 2, sqb=cumsq,
                                   ready_pe=[(s_act, uk_marks[c - 2])])
            wq_gate = [pm13, pm]
            wr_pair_guard[g % 2] = pm
            for sc in (DC - 2, DC - 1):
                stat_mark = ln_stats_chunk(AR2, sc, sqb=cumsq,
                                           ready_pe=[(s_act, uk_marks[sc])])
            if g + 1 < NT:
                emit_corr(g + 1)
            bc2 = ln_finish(stat_mark)
            wt("vector", s_act, bc2)
            wt("vector", s_x, x_mark)
            for c in range(DC):
                ev("vector", lambda e, c=c: e.tensor_mul(
                    t_sb[:], cs(AR2, c), bcr_sb[:]))
                ev("vector", lambda e, c=c: e.tensor_sub(
                    t_sb[:], t_sb[:], bcmr_sb[:]))
                xm = inc("vector", s_dve, lambda e, c=c: e.tensor_add(
                    cs(AR2, c), t_sb[:], cs(AR1, c)))
                wt("scalar", s_dve, xm)
                sm = inc("scalar", s_sp[c % 2], lambda e, g=g, c=c:
                         e.dma_start(out=x1_sp[g, c], in_=cs(AR2, c)), 16)
                att_evac_guard[c] = sm
            xbuf_guard = xm
            ln_state["bcr_free"] = xm
            x1_done[g] = (s_sp[0].n, s_sp[1].n)
        wr_guard = max(wr_pair_guard)

        P.run(nc)
        P = Prog()

        ytg = [AR3, AR4]
        acc = [AR1, AR2]
        mid = [AR6, AR5]
        y_ready = [0, 0]
        acc_zero = [0, 0]
        w13_ready = {}
        w13_free = [s_pe.n, s_pe.n]
        w13_pc = [0]
        w2_ready = {}
        w2_free = [0, 0, 0]
        w2_pc = [0]
        row_stage_guard = 0
        wrow_guard = 0
        out_guard = [0, 0]
        silu_guard = [0, 0]
        silu_bf = [p_pp[0], p_pp[1]]
        tmp_bf = [r_pp[0], r_pp[1]]
        ps_guard_g = [0, 0]
        ps_guard_u = [0, 0]
        accg = [[0, 0], [0, 0]]
        rt_ps_guard = 0
        tr_ps_guard = 0
        tr_pe_guard = 0

        for h in range(2):
            wt("sync", s_sp[0], s_sp[0].n)
            wt("sync", s_sp[1], s_sp[1].n)
            if h == 0:
                wt("sync", s_pe, wr_guard)
                wt("tensor", s_dve, s_dve.n)
                wt("tensor", s_act, s_act.n)
                wt("scalar", s_sp[0], s_sp[0].n)
                wt("scalar", s_sp[1], s_sp[1].n)
            else:
                wt("tensor", s_act, max(ps_guard_g))
                wt("tensor", s_act, ln_state["bc_guard"])
                wt("tensor", s_dve, max(accg[0] + accg[1]))
            ytg_free = wr_guard if h == 0 else max(w13_free)
            for tg in range(2):
                g = 2 * h + tg
                if h == 0:
                    wt("sync", s_sp[0], x1_done[NT - 1][0])
                    wt("sync", s_sp[1], x1_done[NT - 1][1])
                    wt("sync", s_dve, xbuf_guard)
                else:
                    wt("sync", s_sp[0], x1_done[g][0])
                    wt("sync", s_sp[1], x1_done[g][1])
                    wt("sync", s_dve, max(ld_guard))
                x1ld = inc("sync", s_x, lambda e, g=g, tg=tg: e.dma_start(
                    out=acc[tg][:].rearrange("p (c j) -> p c j", c=DC),
                    in_=x1_sp[g].rearrange("c p j -> p c j")), 16)
                act_rdy = [(s_x, x1ld)]
                if tg == 1:
                    act_rdy.append((s_dve, y_ready[0]))
                else:
                    act_rdy.append((s_dve, max(ld_guard[0], ld_guard[1],
                                               xbuf_guard)))
                bc3 = emit_ln(acc[tg], (s_x, x1ld), act_rdy)
                wt("vector", s_act, bc3)
                wt("vector", s_pe, ytg_free)
                for c in range(DC):
                    ev("vector", lambda e, c=c, tg=tg: e.tensor_mul(
                        t_sb[:], cs(acc[tg], c), bcr_sb[:]))
                    fn = lambda e, c=c, tg=tg: e.tensor_sub(
                        cs(ytg[tg], c), t_sb[:], bcmr_sb[:])
                    if c == DC - 1:
                        ym = inc("vector", s_dve, fn)
                    else:
                        ev("vector", fn)
                ln_state["bcr_free"] = ym
                y_ready[tg] = ym
                acc_zero[tg] = inc("vector", s_dve, lambda e, tg=tg: e.memset(
                    acc[tg][:], 0.0))
            for grp in range(8):
                tg, j0 = grp // 4, (grp % 4) * 128
                wt("tensor", s_dve, y_ready[tg])
                wt("tensor", s_dve, rt_ps_guard)
                for kc in range(DC):
                    mm = lambda e, tg=tg, j0=j0, kc=kc: e.matmul(
                        psb[0][:, 0:8],
                        ytg[tg][:, kc * TILE + j0:kc * TILE + j0 + 128],
                        gate_bf[:, kc * 8:(kc + 1) * 8],
                        start=(kc == 0), stop=(kc == DC - 1))
                    if kc == DC - 1:
                        inc("tensor", s_pe, mm)
                    else:
                        ev("tensor", mm)
                for kc in range(DC):
                    mm = lambda e, tg=tg, j0=j0, kc=kc: e.matmul(
                        psb[1][:, 0:1],
                        ytg[tg][:, kc * TILE + j0:kc * TILE + j0 + 128],
                        sgw_bf[:, kc:kc + 1],
                        start=(kc == 0), stop=(kc == DC - 1))
                    if kc == DC - 1:
                        lg_m = inc("tensor", s_pe, mm)
                    else:
                        ev("tensor", mm)
                wt("vector", s_pe, lg_m)
                wt("vector", s_pe, tr_pe_guard)
                wt("scalar", s_pe, tr_pe_guard)

                def dv(fn):
                    m = inc("vector", s_dve, fn)
                    wt("vector", s_dve, m)
                    return m

                dv(lambda e: e.reduce_max(rowsc[:, 0:1], psb[0][:, 0:8],
                                          axis=AX.X))
                m2 = dv(lambda e: e.tensor_single_scalar(
                    probs[:], psb[0][:, 0:8], rowsc[:, 0:1], ALU.subtract))
                wt("scalar", s_dve, m2)
                pex = inc("scalar", s_act, lambda e: e.activation(
                    probs[:], probs[:], AF.Exp))
                m3 = inc("scalar", s_act, lambda e: e.activation(
                    sgl[:], psb[1][:, 0:1], AF.Exp, scale=-1.0))
                wt("vector", s_act, m3)
                dv(lambda e: e.reduce_sum(rowsc[:, 0:1], probs[:], axis=AX.X))
                dv(lambda e: e.tensor_scalar(sgl[:], sgl[:], 1.0, None,
                                             ALU.add))
                dv(lambda e: e.reciprocal(sgl[:], sgl[:]))
                dv(lambda e: e.reciprocal(rowsc[:, 1:2], rowsc[:, 0:1]))
                dv(lambda e: e.tensor_single_scalar(
                    probs[:], probs[:], rowsc[:, 1:2], ALU.mult))
                dv(lambda e: e.reduce_max(rowsc[:, 0:1], probs[:], axis=AX.X))
                dv(lambda e: e.tensor_single_scalar(
                    pr_sc[:], probs[:], rowsc[:, 0:1], ALU.is_lt))
                dv(lambda e: e.tensor_mul(pr_sc[:], pr_sc[:], probs[:]))
                dv(lambda e: e.reduce_max(rowsc[:, 1:2], pr_sc[:], axis=AX.X))
                dv(lambda e: e.tensor_single_scalar(
                    pr_sc[:], probs[:], rowsc[:, 1:2], ALU.is_ge))
                rt_ps_guard = dv(lambda e: e.tensor_mul(
                    pr_sc[:], pr_sc[:], probs[:]))
                wt("tensor", s_dve, rt_ps_guard)
                wt("tensor", s_act, tr_ps_guard)
                for ee in range(8):
                    ev("tensor", lambda e, ee=ee, pb=psb[2 + ee // 4]:
                       e.transpose(pb[0:1, (ee % 4) * 128:(ee % 4) * 128 + 128],
                                   pr_sc[:, ee:ee + 1], ident[:]))
                tr_pe_guard = inc("tensor", s_pe, lambda e: e.transpose(
                    psb[4][0:1, 0:128], sgl[:, 0:1], ident[:]))
                wt("scalar", s_pe, tr_pe_guard)
                wt("scalar", s_row, row_stage_guard)
                for ee in range(8):
                    ev("scalar", lambda e, ee=ee, pb=psb[2 + ee // 4]: e.copy(
                        stage_row[:, ee * 128:(ee + 1) * 128],
                        pb[0:1, (ee % 4) * 128:(ee % 4) * 128 + 128]))
                tr_ps_guard = inc("scalar", s_act, lambda e: e.copy(
                    stage_row[:, 8 * 128:9 * 128], psb[4][0:1, 0:128]))
                wt("sync", s_act, tr_ps_guard)
                for ee in range(9):
                    row_stage_guard = inc(
                        "sync", s_row, lambda e, ee=ee, grp=grp: e.dma_start(
                            out=wrow_dram[ee, grp * 128:(grp + 1) * 128]
                            .rearrange("(o x) -> o x", o=1),
                            in_=stage_row[:, ee * 128:(ee + 1) * 128]), 16)
            router_done = row_stage_guard

            for ei in range(NEXP):
                wr_i = min(ei, 8)
                wt("sync", s_row, router_done)
                wt("sync", s_pe, wrow_guard)
                rw = inc("sync", s_row, lambda e, wr_i=wr_i: e.dma_start(
                    out=wrow1[:], in_=wrow_dram[wr_i].rearrange("(o x) -> o x", o=1)),
                    16)
                wt("tensor", s_row, rw)
                wt("tensor", s_dve, accg[0][0])
                wt("tensor", s_dve, accg[0][1])
                wt("tensor", s_act, ln_state["bc_guard"])
                ev("tensor", lambda e: e.matmul(
                    psb[4][:], ones_row[:], wrow1[:, 0:512], start=True,
                    stop=True))
                btm = inc("tensor", s_pe, lambda e: e.matmul(
                    psb[5][:], ones_row[:], wrow1[:, 512:1024], start=True,
                    stop=True))
                wrow_guard = btm
                wt("vector", s_pe, btm)
                ev("vector", lambda e: e.tensor_copy(wtil[:, 0:512], psb[4][:]))
                wv_m = inc("vector", s_dve, lambda e: e.tensor_copy(
                    wtil[:, 512:1024], psb[5][:]))
                accg[0] = [wv_m, wv_m]
                def stream_w13(jb, ei=ei):
                    par = w13_pc[0] % 2
                    w13_pc[0] += 1
                    wt("gpsimd", s_pe, w13_free[par])
                    inc("gpsimd", s_w[par], lambda e, ei=ei, jb=jb, par=par:
                        e.dma_start(out=wq[par][:, 0:DC * 128],
                                    in_=w13T[ei, jb]), 16)
                    wm = inc("gpsimd", s_w[par],
                             lambda e, ei=ei, jb=jb, par=par: e.dma_start(
                                 out=wq[par][:, DC * 128:DC * 256],
                                 in_=w13T[ei, JC + jb]), 16)
                    w13_ready[jb] = (par, wm)

                def stream_w2(mb, ei=ei):
                    par = w2_pc[0] % 3
                    w2_pc[0] += 1
                    wt("gpsimd", s_pe, w2_free[par])
                    wm = inc("gpsimd", s_w2[par],
                             lambda e, ei=ei, mb=mb, par=par: e.dma_start(
                                 out=w2q[par][:], in_=w2T[ei, mb]),
                             16)
                    w2_ready[mb] = (par, wm)

                stream_w13(0)
                stream_w13(1)
                stream_w2(0)
                stream_w2(1)
                stream_w2(2)
                for jb in range(JC):
                    par, wm = w13_ready[jb]
                    wt("tensor", s_w[par], wm)
                    wt("tensor", s_act, ps_guard_g[0])
                    wt("tensor", s_act, ps_guard_g[1])
                    for kc in range(DC):
                        for tg in range(2):
                            mm = lambda e, tg=tg, kc=kc, par=par: \
                                e.matmul(
                                    psb[tg][:],
                                    wq[par][:, kc * 128:(kc + 1) * 128],
                                    cs(ytg[tg], kc), start=(kc == 0),
                                    stop=(kc == DC - 1))
                            if kc == DC - 1 and tg == 1:
                                gm_ = inc("tensor", s_pe, mm)
                            else:
                                ev("tensor", mm)
                    wt("tensor", s_dve, ps_guard_u[0])
                    wt("tensor", s_dve, ps_guard_u[1])
                    for kc in range(DC):
                        for tg in range(2):
                            mm = lambda e, tg=tg, kc=kc, par=par: \
                                e.matmul(
                                    psb[2 + tg][:],
                                    wq[par][:, DC * 128 + kc * 128:
                                            DC * 128 + (kc + 1) * 128],
                                    cs(ytg[tg], kc), start=(kc == 0),
                                    stop=(kc == DC - 1))
                            if kc == DC - 1 and tg == 1:
                                um_ = inc("tensor", s_pe, mm)
                            else:
                                ev("tensor", mm)
                    w13_free[par] = um_
                    if jb + 2 < JC:
                        stream_w13(jb + 2)
                    wt("scalar", s_pe, gm_)
                    for tg in range(2):
                        wt("scalar", s_dve, silu_guard[tg])
                        sm_ = inc("scalar", s_act,
                                  lambda e, tg=tg: e.activation(
                                      silu_bf[tg][:], psb[tg][:], AF.Silu))
                        ps_guard_g[tg] = sm_
                    wt("vector", s_pe, um_)
                    for tg in range(2):
                        wt("vector", s_act, ps_guard_g[tg])
                        ev("vector", lambda e, tg=tg: e.tensor_mul(
                            tmp_bf[tg][:], psb[2 + tg][:], silu_bf[tg][:]))
                        dm_ = inc("vector", s_dve,
                                  lambda e, jb=jb, tg=tg: e.tensor_mul(
                                      cs(mid[tg], jb), tmp_bf[tg][:],
                                      wtil[:, tg * 512:(tg + 1) * 512]))
                        ps_guard_u[tg] = dm_
                        silu_guard[tg] = dm_
                mid_mark = dm_
                wt("tensor", s_dve, mid_mark)
                accg[1] = [max(accg[1][0], mid_mark),
                           max(accg[1][1], mid_mark)]
                for mb in range(DC):
                    pair = mb % 2
                    bank = 4 if pair == 0 else 2
                    par, wm = w2_ready[mb]
                    wt("tensor", s_w2[par], wm)
                    wt("tensor", s_dve, accg[pair][0])
                    wt("tensor", s_dve, accg[pair][1])
                    for jc in range(JC):
                        for tg in range(2):
                            mm = lambda e, tg=tg, jc=jc, par=par, bank=bank: \
                                e.matmul(
                                    psb[bank + tg][:],
                                    w2q[par][:, jc * 128:(jc + 1) * 128],
                                    mid[tg][:, jc * TILE:(jc + 1) * TILE],
                                    start=(jc == 0), stop=(jc == JC - 1))
                            if jc == JC - 1 and tg == 1:
                                om_ = inc("tensor", s_pe, mm)
                            else:
                                ev("tensor", mm)
                    w2_free[par] = om_
                    if mb + 3 < DC:
                        stream_w2(mb + 3)
                    wt("vector", s_pe, om_)
                    for tg in range(2):
                        wt("vector", s_dve, acc_zero[tg])
                        am_ = inc("vector", s_dve,
                                  lambda e, mb=mb, tg=tg, bank=bank:
                                  e.tensor_add(
                                      cs(acc[tg], mb), cs(acc[tg], mb),
                                      psb[bank + tg][:]))
                        accg[pair][tg] = am_
                ps_guard_u = [max(ps_guard_u[0], accg[1][0]),
                              max(ps_guard_u[1], accg[1][1])]
                acc_mark = am_
            if debug and h == 0:
                wt("sync", s_dve, acc_mark)
                inc("sync", s_out, lambda e: e.dma_start(
                    out=dbg_acc[0], in_=acc[0][:]), 16)
                inc("sync", s_out, lambda e: e.dma_start(
                    out=dbg_acc[1], in_=acc[1][:]), 16)
            for tg in range(2):
                g = 2 * h + tg
                rdy = max(accg[1][tg], accg[0][0], accg[0][1],
                          ld_guard[0], ld_guard[1], y_ready[0], y_ready[1])
                bc4 = emit_ln(acc[tg], (s_dve, rdy), (s_dve, rdy))
                wt("vector", s_act, bc4)
                for c in range(DC):
                    par = c % 2
                    wt("sync", s_dve, ld_guard[par])
                    xm = inc("sync", s_ld[par], lambda e, g=g, c=c, par=par:
                             e.dma_start(out=a_pp[par][:], in_=x1_sp[g, c]),
                             16)
                    ev("vector", lambda e, c=c, tg=tg: e.tensor_mul(
                        t_sb[:], cs(acc[tg], c), bcr_sb[:]))
                    ev("vector", lambda e, c=c: e.tensor_sub(
                        t_sb[:], t_sb[:], bcmr_sb[:]))
                    wt("vector", s_ld[par], xm)
                    wt("vector", s_out, out_guard[par])
                    om2 = inc("vector", s_dve, lambda e, c=c, par=par:
                              e.tensor_add(b_pp[par][:], t_sb[:],
                                           a_pp[par][:]))
                    ld_guard[par] = om2
                    wt("sync", s_dve, om2)
                    out_guard[par] = inc(
                        "sync", s_out, lambda e, g=g, c=c, par=par:
                        e.dma_start(out=out_dram[g, c], in_=b_pp[par][:]), 16)
                ln_state["bcr_free"] = om2

        if debug:
            wt("sync", s_dve, comb_done)
            inc("sync", s_out, lambda e: e.dma_start(
                out=dbg[:, 0:48], in_=summ[:]), 16)
            inc("sync", s_out, lambda e: e.dma_start(
                out=dbg[:, 48:64], in_=a_in[:]), 16)
            inc("sync", s_out, lambda e: e.dma_start(
                out=dbg[:, 64:80], in_=b_in[:]), 16)
        wt("sync", s_out, s_out.n)
        P.run(nc)

    return nc


def _stage_inputs(inputs):
    x = np.asarray(inputs["L0_attn_x"], dtype=np.float32)

    def T(a):
        return np.ascontiguousarray(np.asarray(a, dtype=np.float32).T)

    def vec16(a):
        return np.ascontiguousarray(
            np.asarray(a, dtype=np.float32).reshape(DC, 128).T)

    def wstage(wT):
        return np.ascontiguousarray(
            wT.reshape(DC, 128, 8, 256).transpose(2, 1, 0, 3)
            .reshape(8, 128, DC * 256))

    common = {
        "wkT": wstage(T(inputs["L0_attn_Wk"])),
        "wvT": wstage(T(inputs["L0_attn_Wv"])),
        "wrT": wstage(T(inputs["L0_attn_Wr"])),
        "woT": wstage(T(inputs["L0_attn_Wo"])),
        "bk": vec16(inputs["L0_attn_bk"]), "bv": vec16(inputs["L0_attn_bv"]),
        "br": vec16(inputs["L0_attn_br"]), "bo": vec16(inputs["L0_attn_bo"]),
        "tmk": vec16(inputs["L0_attn_tmk"]), "tmv": vec16(inputs["L0_attn_tmv"]),
        "tmr": vec16(inputs["L0_attn_tmr"]), "uu": vec16(inputs["L0_attn_u"]),
        "ww": vec16(inputs["L0_attn_w"]),
        "ident": np.eye(128, dtype=np.float32),
        "gateT": np.ascontiguousarray(
            T(inputs["L0_ffn_gate_w"]).reshape(DC, 128, 8)),
        "sgwT": vec16(inputs["L0_ffn_shared_gate_w"]),
    }
    w13 = np.asarray(inputs["L0_ffn_w13"], dtype=np.float32)
    w2 = np.asarray(inputs["L0_ffn_w2"], dtype=np.float32)
    sguT = np.asarray(inputs["L0_ffn_shared_gate_up_w"], dtype=np.float32).T
    sdwT = np.asarray(inputs["L0_ffn_shared_down_w"], dtype=np.float32).T
    w13T = np.empty((NEXP, D, 2 * JDIM), np.float32)
    w2T = np.empty((NEXP, JDIM, D), np.float32)
    for e in range(8):
        w13T[e] = w13[e].T
        w2T[e] = w2[e].T
    for q in range(4):
        w13T[8 + q, :, 0:JDIM] = sguT[:, q * JDIM:(q + 1) * JDIM]
        w13T[8 + q, :, JDIM:] = sguT[:, 5632 + q * JDIM:5632 + (q + 1) * JDIM]
        w2T[8 + q] = sdwT[q * JDIM:(q + 1) * JDIM, :]
    common["w13T"] = np.ascontiguousarray(
        w13T.reshape(NEXP, DC, 128, 2 * JC, 128).transpose(0, 3, 2, 1, 4)
        .reshape(NEXP, 2 * JC, 128, DC * 128))
    common["w2T"] = np.ascontiguousarray(
        w2T.reshape(NEXP, JC, 128, DC, 128).transpose(0, 3, 2, 1, 4)
        .reshape(NEXP, DC, 128, JC * 128))

    per_core = []
    for core in range(8):
        b, ch = core // 4, core % 4
        s = ch * TPC
        xTc = x[b, s:s + TPC].T
        xS = np.ascontiguousarray(
            xTc.reshape(DC, 128, NT, TILE).transpose(2, 1, 0, 3)
            .reshape(NT, 128, DC * TILE))
        if s > 0:
            xl = np.ascontiguousarray(x[b, s - 1].reshape(DC, 128).T)
        else:
            xl = np.zeros((128, DC), np.float32)
        ohv = np.zeros((128, 8), np.float32)
        ohv[:, core] = 1.0
        m = {"xT": xS, "xlead_in": xl, "oh": ohv}
        m.update(common)
        per_core.append(m)
    return per_core


def kernel(**inputs):
    per_core = _stage_inputs(inputs)
    nc = build_program()
    res = run_bass_kernel_spmd(nc, per_core, core_ids=list(range(8)))
    out = np.empty((2, 8192, D), np.float32)
    for core in range(8):
        b, ch = core // 4, core % 4
        s = ch * TPC
        oc = res.results[core]["out"]
        out[b, s:s + TPC] = oc.transpose(0, 3, 1, 2).reshape(TPC, D)
    return out
